# version 08
# baseline (speedup 1.0000x reference)
import math
import numpy as np
import ml_dtypes
import concourse.bass as bass
import concourse.mybir as mybir
from concourse.bass_utils import run_bass_kernel_spmd

F32 = mybir.dt.float32
BF16 = mybir.dt.bfloat16
U16 = mybir.dt.uint16
I32 = mybir.dt.int32
AF = mybir.ActivationFunctionType
ALU = mybir.AluOpType
AX = mybir.AxisListType

T = 2048
DM = 2048
NH, NKV, HDIM = 16, 4, 128
TC = 512
NT = T // TC
NC = DM // 128
INVSQ = 1.0 / math.sqrt(128.0)
LN_EPS = 1e-5
BF = ml_dtypes.bfloat16


def host_constants():
    d = {}
    d["idn_f32"] = np.eye(128, dtype=np.float32)
    d["idn_bf"] = np.eye(128, dtype=BF)
    R = np.zeros((128, 128), dtype=np.float64)
    for i in range(128):
        R[i, (i + 64) % 128] = 1.0
    d["rswap_bf"] = R.astype(BF)
    inv = 1.0 / 10000.0 ** (np.arange(0, 128, 2) / 128.0)
    t = np.arange(T, dtype=np.float64)
    fr = np.outer(inv, t)
    cos, sin = np.cos(fr), np.sin(fr)
    d["cosT"] = np.concatenate([cos, cos], 0).astype(np.float32)
    d["sinTs"] = np.concatenate([-sin, sin], 0).astype(np.float32)
    m = (np.arange(128)[None, :] >= np.arange(128)[:, None])
    d["mask_bf"] = m.astype(BF)
    d["ones_col_bf"] = np.ones((128, 1), dtype=BF)
    d["ones_row_bf"] = np.ones((1, 128), dtype=BF)
    d["ones_col_f32"] = np.ones((128, 1), dtype=np.float32)
    d["ones_row_f32"] = np.ones((1, 128), dtype=np.float32)
    d["eps_f32"] = np.full((1, 1), LN_EPS, dtype=np.float32)
    return d


CONST_DTYPES = {"idn_f32": F32, "idn_bf": BF16, "rswap_bf": BF16, "cosT": F32,
                "sinTs": F32, "mask_bf": BF16, "ones_col_bf": BF16,
                "ones_row_bf": BF16, "ones_col_f32": F32, "ones_row_f32": F32,
                "eps_f32": F32}
CONST_SHAPES = {"idn_f32": [128, 128], "idn_bf": [128, 128], "rswap_bf": [128, 128],
                "cosT": [128, T], "sinTs": [128, T], "mask_bf": [128, 128],
                "ones_col_bf": [128, 1], "ones_row_bf": [1, 128],
                "ones_col_f32": [128, 1], "ones_row_f32": [1, 128], "eps_f32": [1, 1]}


class Fillers:
    def __init__(self):
        self.q = []

    def run(self, n=1):
        k = 0
        while self.q and k < n:
            self.q.pop(0)()
            k += 1

    def drain(self):
        while self.q:
            self.q.pop(0)()


class Ctx:
    def __init__(self, nc):
        self.nc = nc
        self.sem = {}
        self.cnt = {}
        for name in ("ld", "wdma", "st", "pe", "act", "dve", "pool"):
            self.sem[name] = nc.alloc_semaphore("s_" + name)
            self.cnt[name] = 0

    def inc(self, name, instr, k=1):
        instr.then_inc(self.sem[name], k)
        self.cnt[name] += k
        return self.cnt[name]

    def wait(self, eng, name, val):
        if val > 0:
            eng.wait_ge(self.sem[name], val)

    def dgate(self, dve, sb):
        return self.inc("dve", dve.tensor_copy(sb["dly"][0:1, 32:64],
                                               sb["dly"][0:1, 0:32]))

    def fence(self, eng, name, instr):
        self.inc(name, instr)
        eng.wait_ge(self.sem[name], self.cnt[name])

    def base(self):
        return dict(self.cnt)


def alloc_common(nc):
    sb = {}
    sb["xbT"] = nc.alloc_sbuf_tensor("xbT", [128, NC * T], BF16)
    sb["prhs"] = nc.alloc_sbuf_tensor("prhs", [128, NC * TC], BF16)
    un1 = nc.alloc_sbuf_tensor("un1", [128, 2 * NC * TC], BF16)
    sb["un1"] = un1
    ap1 = un1[:]
    sb["kT"] = ap1[:, 0:NKV * T]
    sb["V"] = ap1[:, NKV * T:2 * NKV * T]
    sb["resid"] = ap1.bitcast(F32)
    sb["wslab0"] = nc.alloc_sbuf_tensor("wslab0", [128, NC * TC], BF16)
    sb["wslab1"] = nc.alloc_sbuf_tensor("wslab1", [128, NC * TC], BF16)
    sb["qh0"] = nc.alloc_sbuf_tensor("qh0", [128, T], BF16)
    sb["pt0"] = nc.alloc_sbuf_tensor("pt0", [128, TC], BF16)
    sb["pt1"] = nc.alloc_sbuf_tensor("pt1", [128, TC], BF16)
    sb["fbc"] = nc.alloc_sbuf_tensor("fbc", [128, TC], BF16)
    sb["gbc"] = nc.alloc_sbuf_tensor("gbc", [128, TC], BF16)
    sb["ao0"] = nc.alloc_sbuf_tensor("ao0", [128, TC], BF16)
    sb["ao1"] = nc.alloc_sbuf_tensor("ao1", [128, TC], BF16)
    sb["kbst0"] = nc.alloc_sbuf_tensor("kbst0", [128, TC], BF16)
    sb["kbst1"] = nc.alloc_sbuf_tensor("kbst1", [128, TC], BF16)
    sb["sc1"] = nc.alloc_sbuf_tensor("sc1", [128, TC], F32)
    sb["sc2"] = nc.alloc_sbuf_tensor("sc2", [128, TC], F32)
    sb["gs"] = nc.alloc_sbuf_tensor("gs", [16, T], BF16)
    sb["rows"] = nc.alloc_sbuf_tensor("rows", [1, 3 * TC], F32)
    sb["rstd_bc"] = nc.alloc_sbuf_tensor("rstd_bc", [128, TC], F32)
    sb["mrow_bc"] = nc.alloc_sbuf_tensor("mrow_bc", [128, TC], F32)
    sb["xt_ring"] = nc.alloc_sbuf_tensor("xt_ring", [128, 4 * TC], F32)
    sb["y_ring"] = nc.alloc_sbuf_tensor("y_ring", [128, 2 * TC], F32)
    sb["wg16"] = nc.alloc_sbuf_tensor("wg16", [128, 2 * NC * 16], BF16)
    sb["lutbc"] = nc.alloc_sbuf_tensor("lutbc", [128, 16], F32)
    sb["scl"] = nc.alloc_sbuf_tensor("scl", [128, 512], F32)
    sb["dly"] = nc.alloc_sbuf_tensor("dly", [1, 64], F32)
    for name in CONST_DTYPES:
        sb[name] = nc.alloc_sbuf_tensor("sb_" + name, CONST_SHAPES[name],
                                        CONST_DTYPES[name])
    psum = [nc.alloc_psum_tensor(f"p{i}", [128, TC], F32) for i in range(8)]
    return sb, psum


def declare_consts(nc):
    return {name: nc.dram_tensor("c_" + name, CONST_SHAPES[name], dt,
                                 kind="ExternalInput").ap()
            for name, dt in CONST_DTYPES.items()}


def load_consts(ctx, block, cap, sb):
    @block.sync
    def _(sync):
        for name in CONST_DTYPES:
            ctx.inc("ld", sync.dma_start(out=sb[name][:], in_=cap[name][:, :]), 16)


def proj_resid_ln(ctx, block, sb, psum, src_dram, w_bf_dram, xT_src, store_y,
                  y_bf_dst=None):
    proj_grp_pe = []
    proj_grp_dve = []
    for j in range(NT):
        b = ctx.base()

        prhs_pe_tgt = proj_grp_pe[j * 4 - 1] if j >= 1 else None

        @block.sync
        def _(sync, j=j, b=b, prhs_pe_tgt=prhs_pe_tgt):
            ctx.wait(sync, "pe", b["pe"] if prhs_pe_tgt is None else prhs_pe_tgt)
            ctx.wait(sync, "st", b["st"])
            for c in range(NC):
                ctx.inc("ld", sync.dma_start(
                    out=sb["prhs"][:, c * TC:(c + 1) * TC],
                    in_=src_dram[c * 128:(c + 1) * 128, j * TC:(j + 1) * TC]), 16)

        for hg in range(4):
            slab = sb["wslab0"] if (j * 4 + hg) % 2 == 0 else sb["wslab1"]
            b = ctx.base()
            g = j * 4 + hg
            pe_tgt = proj_grp_pe[g - 2] if g >= 2 else b["pe"]
            dve_tgt = proj_grp_dve[g - 1] if g >= 1 else b["dve"]

            @block.sync
            def _(sync, j=j, hg=hg, slab=slab, b=b, pe_tgt=pe_tgt, dve_tgt=dve_tgt):
                ctx.wait(sync, "pe", pe_tgt)
                ctx.wait(sync, "dve", dve_tgt)
                for c in range(NC):
                    ctx.inc("ld", sync.dma_start(
                        out=slab[:, c * TC:(c + 1) * TC],
                        in_=w_bf_dram[hg * 512:(hg + 1) * 512, c * 128:(c + 1) * 128],
                        transpose=True), 16)
                for u in range(4):
                    q = hg * 4 + u
                    ctx.inc("ld", sync.dma_start(
                        out=sb["xt_ring"][:, u * TC:(u + 1) * TC],
                        in_=xT_src[q * 128:(q + 1) * 128, j * TC:(j + 1) * TC]), 16)

            b2 = ctx.base()

            @block.tensor
            def _(pe, j=j, hg=hg, slab=slab, b2=b2):
                ctx.wait(pe, "ld", b2["ld"])
                ctx.wait(pe, "dve", b2["dve"])
                for u in range(4):
                    if u >= 2:
                        ctx.wait(pe, "dve", b2["dve"] + u - 1)
                    for c in range(NC):
                        ins = pe.matmul(psum[u % 2][:],
                                        slab[:, c * TC + u * 128:c * TC + (u + 1) * 128],
                                        sb["prhs"][:, c * TC:(c + 1) * TC],
                                        start=(c == 0), stop=(c == NC - 1))
                    ctx.inc("pe", ins)

            b3 = ctx.base()

            @block.vector
            def _(dve, j=j, hg=hg, b3=b3):
                for u in range(4):
                    q = hg * 4 + u
                    ctx.wait(dve, "pe", b3["pe"] - 4 + u + 1)
                    ctx.inc("dve", dve.tensor_tensor(
                        sb["resid"][:, q * TC:(q + 1) * TC], psum[u % 2][:],
                        sb["xt_ring"][:, u * TC:(u + 1) * TC], ALU.add))

            b4 = ctx.base()
            pe_stats_base = ctx.cnt["pe"]

            @block.scalar
            def _(act, j=j, hg=hg, b4=b4, pe_stats_base=pe_stats_base):
                for u in range(4):
                    q = hg * 4 + u
                    ctx.wait(act, "dve", b4["dve"] - 4 + u + 1)
                    if u >= 2:
                        ctx.wait(act, "pe", pe_stats_base + (u - 2) + 1)
                    ctx.inc("act", act.activation(
                        (sb["sc1"] if u % 2 == 0 else sb["sc2"])[:],
                        sb["resid"][:, q * TC:(q + 1) * TC], AF.Square))

            b5 = ctx.base()

            @block.tensor
            def _(pe, j=j, hg=hg, b5=b5):
                for u in range(4):
                    q = hg * 4 + u
                    ctx.wait(pe, "dve", b5["dve"] - 4 + u + 1)
                    i1 = pe.matmul(psum[2][0:1, :], sb["ones_col_f32"][:],
                                   sb["resid"][:, q * TC:(q + 1) * TC],
                                   start=(q == 0), stop=(q == NC - 1))
                    ctx.wait(pe, "act", b5["act"] - 4 + u + 1)
                    i2 = pe.matmul(psum[3][0:1, :], sb["ones_col_f32"][:],
                                   (sb["sc1"] if u % 2 == 0 else sb["sc2"])[:],
                                   start=(q == 0), stop=(q == NC - 1))
                    ctx.inc("pe", i2)

            proj_grp_pe.append(ctx.cnt["pe"])
            proj_grp_dve.append(ctx.cnt["dve"])

        rows = sb["rows"]
        b = ctx.base()

        @block.vector
        def _(dve, j=j, b=b):
            ctx.wait(dve, "pe", b["pe"])
            ctx.fence(dve, "dve", dve.tensor_scalar(rows[0:1, 0:TC], psum[2][0:1, :],
                                                    1.0 / DM, None, ALU.mult))
            ctx.fence(dve, "dve", dve.tensor_scalar(rows[0:1, TC:2 * TC],
                                                    psum[3][0:1, :], 1.0 / DM, None,
                                                    ALU.mult))
            ctx.fence(dve, "dve", dve.tensor_tensor(rows[0:1, 2 * TC:3 * TC],
                                                    rows[0:1, 0:TC], rows[0:1, 0:TC],
                                                    ALU.mult))
            ctx.inc("dve", dve.tensor_tensor(rows[0:1, TC:2 * TC],
                                             rows[0:1, TC:2 * TC],
                                             rows[0:1, 2 * TC:3 * TC], ALU.subtract))

        b = ctx.base()

        @block.scalar
        def _(act, j=j, b=b):
            ctx.wait(act, "dve", b["dve"])
            ctx.inc("act", act.activation(rows[0:1, 2 * TC:3 * TC],
                                          rows[0:1, TC:2 * TC], AF.Sqrt,
                                          bias=sb["eps_f32"][0:1, 0:1]))

        b = ctx.base()

        @block.vector
        def _(dve, j=j, b=b):
            ctx.wait(dve, "act", b["act"])
            ctx.fence(dve, "dve", dve.reciprocal(rows[0:1, TC:2 * TC],
                                                 rows[0:1, 2 * TC:3 * TC]))
            ctx.inc("dve", dve.tensor_tensor(rows[0:1, 2 * TC:3 * TC], rows[0:1, 0:TC],
                                             rows[0:1, TC:2 * TC], ALU.mult))

        b = ctx.base()

        @block.tensor
        def _(pe, j=j, b=b):
            ctx.wait(pe, "dve", b["dve"])
            ctx.inc("pe", pe.matmul(psum[4][:], sb["ones_row_f32"][:],
                                    rows[0:1, TC:2 * TC], start=True, stop=True))
            ctx.inc("pe", pe.matmul(psum[5][:], sb["ones_row_f32"][:],
                                    rows[0:1, 2 * TC:3 * TC], start=True, stop=True))

        b = ctx.base()

        @block.scalar
        def _(act, j=j, b=b):
            ctx.wait(act, "pe", b["pe"])
            act.copy(sb["rstd_bc"][:], psum[4][:])
            ctx.inc("act", act.copy(sb["mrow_bc"][:], psum[5][:]))

        b = ctx.base()
        npq = 2 if y_bf_dst is not None else 1

        @block.vector
        def _(dve, j=j, b=b, npq=npq):
            ctx.wait(dve, "act", b["act"])
            for q in range(NC):
                slot = q % 2
                if q >= 2:
                    ctx.wait(dve, "st", b["st"] + (q - 1) * 16)
                dve.tensor_tensor(sb["sc1"][:], sb["resid"][:, q * TC:(q + 1) * TC],
                                  sb["rstd_bc"][:], ALU.mult)
                dve.tensor_tensor(
                    sb["y_ring"][:, slot * TC:(slot + 1) * TC], sb["sc1"][:],
                    sb["mrow_bc"][:], ALU.subtract)
                if y_bf_dst is not None:
                    dve.tensor_copy(
                        y_bf_dst[:, q * T + j * TC:q * T + (j + 1) * TC],
                        sb["y_ring"][:, slot * TC:(slot + 1) * TC])
                ctx.dgate(dve, sb)

        b = ctx.base()

        @block.sync
        def _(sync, j=j, b=b, npq=npq):
            for q in range(NC):
                slot = q % 2
                ctx.wait(sync, "dve", b["dve"] - NC + q + 1)
                ctx.inc("st", store_y(sync, j, q,
                                      sb["y_ring"][:, slot * TC:(slot + 1) * TC]), 16)




def _dump(ctx, block, dbg_ap, src_ap, pad16=False):
    b = ctx.base()

    @block.sync
    def _(sync, b=b):
        ctx.wait(sync, "dve", b["dve"])
        ctx.wait(sync, "act", b["act"])
        ctx.wait(sync, "pe", b["pe"])
        if pad16:
            ctx.inc("st", sync.dma_start(out=dbg_ap[0:16, 0:src_ap.shape[1]],
                                         in_=src_ap), 16)
        else:
            ctx.inc("st", sync.dma_start(out=dbg_ap[:, 0:src_ap.shape[1]],
                                         in_=src_ap), 16)

def build_l0_attn(ctx, block, io, sb, psum, stage=35, filler=None):
    xbT, kT, V = sb["xbT"], sb["kT"], sb["V"]

    @block.gpsimd
    def _(g):
        ctx.inc("wdma", g.dma_start(out=io["wk_bf"][:, :], in_=io["Wk"][:, :]), 16)
        ctx.inc("wdma", g.dma_start(out=io["wv_bf"][:, :], in_=io["Wv"][:, :]), 16)
        ctx.inc("wdma", g.dma_start(out=io["wq_bf"][:, :], in_=io["Wq"][:, :]), 16)
        ctx.inc("wdma", g.dma_start(out=io["wo_bf"][:, :], in_=io["Wo"][:, :]), 16)
        ctx.inc("wdma", g.dma_start(out=io["wgs_bf"][0:16, :], in_=io["Wgslc"][:, :]), 16)
        ctx.inc("wdma", g.dma_start(out=io["wgs_bf"][16:32, :], in_=io["Wgswa"][:, :]), 16)
        for c in range(NC):
            ctx.inc("wdma", g.dma_start(out=xbT[:, c * T:(c + 1) * T],
                                        in_=io["xT_in"][c * 128:(c + 1) * 128, :]), 16)

    b = ctx.base()

    @block.sync
    def _(sync, b=b):
        ctx.wait(sync, "wdma", b["wdma"])
        for w in range(2):
            for c in range(NC):
                ctx.inc("ld", sync.dma_start(
                    out=sb["wg16"][:, (w * NC + c) * 16:(w * NC + c + 1) * 16],
                    in_=io["wgs_bf"][w * 16:(w + 1) * 16, c * 128:(c + 1) * 128],
                    transpose=True), 16)
        for c in range(NC):
            ctx.inc("ld", sync.dma_start(out=sb["wslab0"][:, c * TC:(c + 1) * TC],
                                         in_=io["wk_bf"][0:512, c * 128:(c + 1) * 128],
                                         transpose=True), 16)

    items_k = [(kv, tc) for kv in range(NKV) for tc in range(NT)]
    b = ctx.base()

    @block.tensor
    def _(pe, b=b):
        ctx.wait(pe, "ld", b["ld"])
        ctx.wait(pe, "wdma", b["wdma"])
        for m, (kv, tc) in enumerate(items_k):
            ctx.wait(pe, "act", b["act"] + max(m - 1, 0))
            for c in range(NC):
                ins = pe.matmul(psum[m % 2][:],
                                sb["wslab0"][:, c * TC + kv * 128:c * TC + (kv + 1) * 128],
                                xbT[:, c * T + tc * TC:c * T + (tc + 1) * TC],
                                start=(c == 0), stop=(c == NC - 1))
            ctx.inc("pe", ins)
            ctx.wait(pe, "act", b["act"] + m + 1)
            ctx.wait(pe, "dve", b["dve"] + m - 1)
            ctx.inc("pe", pe.matmul(psum[4 + m % 2][:], sb["rswap_bf"][:],
                                    (sb["kbst0"] if m % 2 == 0 else sb["kbst1"])[:],
                                    start=True, stop=True))

    b2 = ctx.base()

    @block.scalar
    def _(act, b2=b2):
        for m in range(len(items_k)):
            ctx.wait(act, "pe", b2["pe"] - 32 + 2 * m + 1)
            if m >= 2:
                ctx.wait(act, "dve", b2["dve"] + m - 1)
            ctx.inc("act", act.copy((sb["kbst0"] if m % 2 == 0 else sb["kbst1"])[:],
                                    psum[m % 2][:]))

    b3 = ctx.base()

    @block.vector
    def _(dve, b3=b3):
        for m, (kv, tc) in enumerate(items_k):
            ctx.wait(dve, "pe", b3["pe"] - 32 + 2 * m + 2)
            stage = sb["kbst0"] if m % 2 == 0 else sb["kbst1"]
            dve.tensor_tensor(sb["sc1"][:], stage[:],
                              sb["cosT"][:, tc * TC:(tc + 1) * TC], ALU.mult)
            dve.tensor_tensor(sb["sc2"][:], psum[4 + m % 2][:],
                              sb["sinTs"][:, tc * TC:(tc + 1) * TC], ALU.mult)
            ctx.inc("dve", dve.tensor_tensor(kT[:, kv * T + tc * TC:kv * T + (tc + 1) * TC],
                                             sb["sc1"][:], sb["sc2"][:], ALU.add))

    if stage <= 1:
        _dump(ctx, block, io["dbg"], sb["kT"])
        return

    b = ctx.base()

    @block.sync
    def _(sync, b=b):
        for c in range(NC):
            ctx.inc("ld", sync.dma_start(out=sb["wslab1"][:, c * TC:(c + 1) * TC],
                                         in_=io["wv_bf"][0:512, c * 128:(c + 1) * 128],
                                         transpose=True), 16)

    b = ctx.base()

    @block.tensor
    def _(pe, b=b):
        ctx.wait(pe, "ld", b["ld"])
        for st in range(NC):
            ctx.wait(pe, "act", b["act"] + st - 1)
            for c in range(NC):
                ins = pe.matmul(psum[6 + st % 2][:],
                                xbT[:, c * T + st * 128:c * T + (st + 1) * 128],
                                sb["wslab1"][:, c * TC:(c + 1) * TC],
                                start=(c == 0), stop=(c == NC - 1))
            ctx.inc("pe", ins)

    b = ctx.base()

    @block.scalar
    def _(act, b=b):
        for st in range(NC):
            ctx.wait(act, "pe", b["pe"] - NC + st + 1)
            ctx.inc("act", act.copy(V[:, st * TC:(st + 1) * TC], psum[6 + st % 2][:]))

    if stage <= 2:
        _dump(ctx, block, io["dbg"], sb["V"])
        return

    ngate_tc = {21: 1, 22: 4}.get(stage, NT)
    skip_gadd = (stage in (21, 22))

    b = ctx.base()

    @block.tensor
    def _(pe, b=b):
        for tc in range(ngate_tc):
            ctx.wait(pe, "act", b["act"] + tc)
            for w in range(2):
                for c in range(NC):
                    ins = pe.matmul(psum[2 + w][0:16, :],
                                    sb["wg16"][:, (w * NC + c) * 16:(w * NC + c + 1) * 16],
                                    xbT[:, c * T + tc * TC:c * T + (tc + 1) * TC],
                                    start=(c == 0), stop=(c == NC - 1))
                ctx.inc("pe", ins)

    b = ctx.base()

    @block.scalar
    def _(act, b=b):
        for tc in range(ngate_tc):
            ctx.wait(act, "pe", b["pe"] - 2 * ngate_tc + 2 * tc + 2)
            if not skip_gadd:
                ctx.wait(act, "dve", b["dve"] + tc)
            act.activation(sb["gs"][0:16, tc * TC:(tc + 1) * TC],
                           psum[2][0:16, :], AF.Sigmoid)
            ctx.inc("act", act.activation(sb["fbc"][0:16, :],
                                          psum[3][0:16, :], AF.Sigmoid))

    b = ctx.base()

    @block.vector
    def _(dve, b=b):
        if skip_gadd:
            return
        for tc in range(ngate_tc):
            ctx.wait(dve, "act", b["act"] - ngate_tc + tc + 1)
            ctx.inc("dve", dve.tensor_tensor(sb["gs"][0:16, tc * TC:(tc + 1) * TC],
                                             sb["gs"][0:16, tc * TC:(tc + 1) * TC],
                                             sb["fbc"][0:16, :], ALU.add))

    if stage <= 23:
        _dump(ctx, block, io["dbg"], sb["gs"][0:16, :], pad16=True)
        return

    attn_dve_mark = [0]
    nst = [0]
    bst0 = ctx.cnt["st"]
    for grp in range(NKV):
        b = ctx.base()

        @block.sync
        def _(sync, grp=grp, b=b):
            ctx.wait(sync, "pe", b["pe"])
            for c in range(NC):
                ctx.inc("ld", sync.dma_start(
                    out=sb["wslab0"][:, c * TC:(c + 1) * TC],
                    in_=io["wq_bf"][grp * 512:(grp + 1) * 512, c * 128:(c + 1) * 128],
                    transpose=True), 16)

        for u in range(4):
            h = grp * 4 + u
            kv = grp
            qh = sb["qh0"]

            b = ctx.base()

            qm = attn_dve_mark[0]

            @block.tensor
            def _(pe, u=u, qh=qh, b=b, qm=qm):
                ctx.wait(pe, "ld", b["ld"])
                ctx.wait(pe, "dve", qm if qm else b["dve"])
                for tc in range(NT):
                    ctx.wait(pe, "act", b["act"] + tc - 1)
                    for c in range(NC):
                        ins = pe.matmul(psum[tc % 2][:],
                                        sb["wslab0"][:, c * TC + u * 128:c * TC + (u + 1) * 128],
                                        xbT[:, c * T + tc * TC:c * T + (tc + 1) * TC],
                                        start=(c == 0), stop=(c == NC - 1))
                    ctx.inc("pe", ins)
                    ctx.wait(pe, "act", b["act"] + tc + 1)
                    ctx.wait(pe, "dve", b["dve"] + tc - 1)
                    ctx.inc("pe", pe.matmul(psum[4 + tc % 2][:], sb["rswap_bf"][:],
                                            (sb["kbst0"] if tc % 2 == 0 else sb["kbst1"])[:],
                                            start=True, stop=True))

            b2 = ctx.base()

            @block.scalar
            def _(act, b2=b2):
                for tc in range(NT):
                    ctx.wait(act, "pe", b2["pe"] - 8 + 2 * tc + 1)
                    if tc >= 2:
                        ctx.wait(act, "dve", b2["dve"] + tc - 1)
                    ctx.inc("act", act.copy((sb["kbst0"] if tc % 2 == 0 else sb["kbst1"])[:],
                                            psum[tc % 2][:]))

            b3 = ctx.base()

            @block.vector
            def _(dve, qh=qh, b3=b3):
                for tc in range(NT):
                    ctx.wait(dve, "pe", b3["pe"] - 8 + 2 * tc + 2)
                    stage = sb["kbst0"] if tc % 2 == 0 else sb["kbst1"]
                    dve.tensor_tensor(sb["sc1"][:], stage[:],
                                      sb["cosT"][:, tc * TC:(tc + 1) * TC], ALU.mult)
                    dve.tensor_tensor(sb["sc2"][:], psum[4 + tc % 2][:],
                                      sb["sinTs"][:, tc * TC:(tc + 1) * TC], ALU.mult)
                    ctx.inc("dve", dve.tensor_tensor(qh[:, tc * TC:(tc + 1) * TC],
                                                     sb["sc1"][:], sb["sc2"][:], ALU.add))
            attn_dve_mark[0] = ctx.cnt["dve"]

            for j in range(NT):
                smax = min(NC, 4 * (j + 1))
                minfo = []
                for i in range(smax):
                    zl = i * 128 - j * TC
                    minfo.append((max(zl, 0), 0 <= zl < TC))
                b = ctx.base()

                cm = attn_dve_mark[0]

                @block.tensor
                def _(pe, kv=kv, j=j, qh=qh, smax=smax, minfo=minfo, b=b, cm=cm):
                    ctx.wait(pe, "dve", cm if cm else b["dve"])
                    dcnt = 0
                    for i in range(smax):
                        zl, hd = minfo[i]
                        ctx.wait(pe, "act", b["act"] + i - 1)
                        ctx.inc("pe", pe.matmul(psum[2 + i % 2][:],
                                                kT[:, kv * T + i * 128:kv * T + (i + 1) * 128],
                                                qh[:, j * TC:(j + 1) * TC],
                                                start=True, stop=True))
                        nm = (1 if zl > 0 else 0) + (1 if hd else 0)
                        if nm:
                            dcnt += nm
                            ctx.wait(pe, "dve", b["dve"] + dcnt)
                        else:
                            ctx.wait(pe, "act", b["act"] + i + 1)
                        pt = sb["pt0"] if i % 2 == 0 else sb["pt1"]
                        pe.matmul(psum[5][:], V[:, i * TC + kv * 128:i * TC + (kv + 1) * 128],
                                  pt[:], start=(i == 0), stop=(i == smax - 1))
                        ctx.inc("pe", pe.matmul(psum[6][0:1, :], sb["ones_col_bf"][:],
                                                pt[:], start=(i == 0), stop=(i == smax - 1)))

                b2 = ctx.base()

                @block.scalar
                def _(act, smax=smax, b2=b2):
                    for i in range(smax):
                        ctx.wait(act, "pe", b2["pe"] - 2 * smax + 2 * i + 1)
                        pt = sb["pt0"] if i % 2 == 0 else sb["pt1"]
                        ctx.inc("act", act.activation(pt[:], psum[2 + i % 2][:],
                                                      AF.Exp, scale=INVSQ))

                b3 = ctx.base()

                @block.vector
                def _(dve, smax=smax, minfo=minfo, b3=b3):
                    for i in range(smax):
                        zl, hd = minfo[i]
                        if zl == 0 and not hd:
                            continue
                        ctx.wait(dve, "act", b3["act"] - smax + i + 1)
                        pt = sb["pt0"] if i % 2 == 0 else sb["pt1"]
                        if zl > 0:
                            ctx.inc("dve", dve.memset(pt[:, 0:zl], 0.0))
                        if hd:
                            ctx.inc("dve", dve.tensor_tensor(pt[:, zl:zl + 128],
                                                             pt[:, zl:zl + 128],
                                                             sb["mask_bf"][:], ALU.mult))

                b = ctx.base()

                @block.vector
                def _(dve, b=b):
                    ctx.wait(dve, "pe", b["pe"])
                    ctx.fence(dve, "dve", dve.tensor_scalar(sb["rows"][0:1, 0:TC],
                                                            psum[6][0:1, :], 1.0, None,
                                                            ALU.mult))
                    ctx.fence(dve, "dve", dve.reciprocal(sb["rows"][0:1, TC:2 * TC],
                                                         sb["rows"][0:1, 0:TC]))
                    ctx.inc("dve", dve.tensor_scalar(sb["fbc"][0:1, :],
                                                     sb["rows"][0:1, TC:2 * TC],
                                                     1.0, None, ALU.mult))

                b = ctx.base()

                @block.tensor
                def _(pe, h=h, j=j, b=b):
                    ctx.wait(pe, "dve", b["dve"])
                    ctx.inc("pe", pe.matmul(psum[7][:], sb["ones_row_bf"][:],
                                            sb["fbc"][0:1, :], start=True, stop=True))
                    ctx.inc("pe", pe.matmul(psum[0][:],
                                            sb["idn_bf"][0:16, h:h + 1].broadcast_to([16, 128]),
                                            sb["gs"][0:16, j * TC:(j + 1) * TC],
                                            start=True, stop=True))

                b = ctx.base()

                @block.scalar
                def _(act, b=b):
                    ctx.wait(act, "pe", b["pe"])
                    act.copy(sb["fbc"][:], psum[7][:])
                    ctx.inc("act", act.copy(sb["gbc"][:], psum[0][:]))

                nastore = nst[0]
                ao = sb["ao0"] if nastore % 2 == 0 else sb["ao1"]
                b = ctx.base()

                @block.vector
                def _(dve, ao=ao, nastore=nastore, b=b):
                    ctx.wait(dve, "act", b["act"])
                    if nastore >= 2:
                        ctx.wait(dve, "st", bst0 + (nastore - 1) * 16)
                    dve.tensor_tensor(sb["sc2"][:], psum[5][:], sb["fbc"][:], ALU.mult)
                    dve.tensor_tensor(ao[:], sb["sc2"][:], sb["gbc"][:], ALU.mult)
                    ctx.dgate(dve, sb)

                b = ctx.base()

                @block.sync
                def _(sync, h=h, j=j, ao=ao, b=b):
                    ctx.wait(sync, "dve", b["dve"])
                    ctx.inc("st", sync.dma_start(
                        out=io["attn_bf"][h * 128:(h + 1) * 128, j * TC:(j + 1) * TC],
                        in_=ao[:]), 16)
                nst[0] += 1
                attn_dve_mark[0] = ctx.cnt["dve"]
                if filler is not None:
                    filler.run(5)

    if stage <= 34:
        return

    if filler is not None:
        filler.drain()

    def store_y(sync, j, q, ysl):
        return sync.dma_start(out=io["xT_out"][q * 128:(q + 1) * 128, j * TC:(j + 1) * TC],
                              in_=ysl)

    proj_resid_ln(ctx, block, sb, psum, io["attn_bf"], io["wo_bf"], io["xT_in"],
                  store_y, y_bf_dst=io.get("y_bf_dst"))


def finish(ctx, block):
    b = ctx.base()

    @block.sync
    def _(sync, b=b):
        ctx.wait(sync, "st", b["st"])
        ctx.wait(sync, "wdma", b["wdma"])
        ctx.wait(sync, "dve", b["dve"])
        ctx.wait(sync, "act", b["act"])
        ctx.wait(sync, "pe", b["pe"])


def build_program_l0_attn(stage=35):
    nc = bass.Bass("TRN2", target_bir_lowering=False, debug=False)
    io = {}
    io["xT_in"] = nc.dram_tensor("xT_in", [DM, T], F32, kind="ExternalInput").ap()
    io["Wq"] = nc.dram_tensor("Wq", [DM, DM], F32, kind="ExternalInput").ap()
    io["Wk"] = nc.dram_tensor("Wk", [512, DM], F32, kind="ExternalInput").ap()
    io["Wv"] = nc.dram_tensor("Wv", [512, DM], F32, kind="ExternalInput").ap()
    io["Wo"] = nc.dram_tensor("Wo", [DM, DM], F32, kind="ExternalInput").ap()
    io["Wgslc"] = nc.dram_tensor("Wgslc", [16, DM], F32, kind="ExternalInput").ap()
    io["Wgswa"] = nc.dram_tensor("Wgswa", [16, DM], F32, kind="ExternalInput").ap()
    io["xT_out"] = nc.dram_tensor("xT_out", [DM, T], F32, kind="ExternalOutput").ap()
    io["wq_bf"] = nc.dram_tensor("wq_bf", [DM, DM], BF16, kind="Internal").ap()
    io["wk_bf"] = nc.dram_tensor("wk_bf", [512, DM], BF16, kind="Internal").ap()
    io["wv_bf"] = nc.dram_tensor("wv_bf", [512, DM], BF16, kind="Internal").ap()
    io["wo_bf"] = nc.dram_tensor("wo_bf", [DM, DM], BF16, kind="Internal").ap()
    io["wgs_bf"] = nc.dram_tensor("wgs_bf", [32, DM], BF16, kind="Internal").ap()
    io["attn_bf"] = nc.dram_tensor("attn_bf", [DM, T], BF16,
                                   kind=("ExternalOutput" if stage == 34 else "Internal")).ap()
    io["dbg"] = nc.dram_tensor("dbg", [128, 2 * NC * TC], BF16, kind="ExternalOutput").ap()
    cap = declare_consts(nc)
    sb, psum = alloc_common(nc)
    ctx = Ctx(nc)
    with nc.Block() as block:
        load_consts(ctx, block, cap, sb)
        build_l0_attn(ctx, block, io, sb, psum, stage=stage)
        finish(ctx, block)
    return nc


def _run_l0_attn(xT_all, weights, stage=35):
    nc = build_program_l0_attn(stage=stage)
    consts = host_constants()
    in_maps = []
    for cid in range(8):
        m = {"xT_in": np.ascontiguousarray(xT_all[cid]),
             "Wq": weights["L0_attn_Wq"], "Wk": weights["L0_attn_Wk"],
             "Wv": weights["L0_attn_Wv"], "Wo": weights["L0_attn_Wo"],
             "Wgslc": weights["L0_attn_Wgslc"], "Wgswa": weights["L0_attn_Wgswa"]}
        for k, v in consts.items():
            m["c_" + k] = v
        in_maps.append(m)
    res = run_bass_kernel_spmd(nc, in_maps, core_ids=list(range(8)))
    if stage >= 35:
        return np.stack([res.results[i]["xT_out"] for i in range(8)])
    key = "attn_bf" if stage == 34 else "dbg"
    return np.stack([res.results[i][key] for i in range(8)])



def build_l1_attn(ctx, block, io, sb, psum):
    xbT = sb["xbT"]
    cT = sb["kT"]
    wvT = sb["V"]
    vh = sb["qh0"]
    QaT = sb["prhs"]
    ah = sb["wslab1"]
    wq_kh = sb["xt_ring"][:].bitcast(BF16)

    @block.gpsimd
    def _(g):
        ctx.inc("wdma", g.dma_start(out=io["wd_bf"][:, :], in_=io["Wd"][:, :]), 16)
        ctx.inc("wdma", g.dma_start(out=io["wv1_bf"][:, :], in_=io["Wv1"][:, :]), 16)
        ctx.inc("wdma", g.dma_start(out=io["wq1_bf"][:, :], in_=io["Wq1"][:, :]), 16)
        ctx.inc("wdma", g.dma_start(out=io["wk1_bf"][:, :], in_=io["Wk1"][:, :]), 16)
        ctx.inc("wdma", g.dma_start(out=io["wo1_bf"][:, :], in_=io["Wo1"][:, :]), 16)
        for c in range(NC):
            ctx.inc("wdma", g.dma_start(out=xbT[:, c * T:(c + 1) * T],
                                        in_=io["xT_in"][c * 128:(c + 1) * 128, :]), 16)

    b = ctx.base()

    @block.sync
    def _(sync, b=b):
        ctx.wait(sync, "wdma", b["wdma"])
        for c in range(NC):
            ctx.inc("ld", sync.dma_start(out=sb["wslab0"][:, c * TC:(c + 1) * TC],
                                         in_=io["wd_bf"][0:512, c * 128:(c + 1) * 128],
                                         transpose=True), 16)
        for kt in range(4):
            ctx.inc("ld", sync.dma_start(out=wvT[:, kt * T:(kt + 1) * T],
                                         in_=io["wv1_bf"][0:T, kt * 128:(kt + 1) * 128],
                                         transpose=True), 16)

    items_c = [(kt, tc) for kt in range(4) for tc in range(NT)]
    b = ctx.base()

    @block.tensor
    def _(pe, b=b):
        ctx.wait(pe, "ld", b["ld"])
        ctx.wait(pe, "wdma", b["wdma"])
        for m, (kt, tc) in enumerate(items_c):
            ctx.wait(pe, "act", b["act"] + m - 1)
            for c in range(NC):
                ins = pe.matmul(psum[m % 2][:],
                                sb["wslab0"][:, c * TC + kt * 128:c * TC + (kt + 1) * 128],
                                xbT[:, c * T + tc * TC:c * T + (tc + 1) * TC],
                                start=(c == 0), stop=(c == NC - 1))
            ctx.inc("pe", ins)

    b = ctx.base()

    @block.scalar
    def _(act, b=b):
        for m, (kt, tc) in enumerate(items_c):
            ctx.wait(act, "pe", b["pe"] - 16 + m + 1)
            ctx.inc("act", act.copy(cT[:, kt * T + tc * TC:kt * T + (tc + 1) * TC],
                                    psum[m % 2][:]))

    for h in range(NH):
        b = ctx.base()

        @block.sync
        def _(sync, h=h, b=b):
            ctx.wait(sync, "pe", b["pe"])
            ctx.inc("ld", sync.dma_start(out=wq_kh[:, 0:T],
                                         in_=io["wq1_bf"][h * 128:(h + 1) * 128, :]), 16)
            ctx.inc("ld", sync.dma_start(out=wq_kh[:, T:T + 512],
                                         in_=io["wk1_bf"][h * 128:(h + 1) * 128, :]), 16)

        b = ctx.base()

        @block.tensor
        def _(pe, h=h, b=b):
            ctx.wait(pe, "ld", b["ld"])
            for dc in range(NC):
                ctx.wait(pe, "act", b["act"] + dc - 1)
                ctx.inc("pe", pe.matmul(psum[dc % 2][:],
                                        wq_kh[:, dc * 128:(dc + 1) * 128],
                                        wq_kh[:, T:T + 512],
                                        start=True, stop=True))

        b = ctx.base()

        @block.scalar
        def _(act, b=b):
            for dc in range(NC):
                ctx.wait(act, "pe", b["pe"] - NC + dc + 1)
                ctx.inc("act", act.copy(ah[:, dc * TC:(dc + 1) * TC], psum[dc % 2][:]))

        items_q = [(kt, tc) for kt in range(4) for tc in range(NT)]
        b = ctx.base()

        @block.tensor
        def _(pe, b=b):
            ctx.wait(pe, "act", b["act"])
            ctx.wait(pe, "dve", b["dve"])
            for m, (kt, tc) in enumerate(items_q):
                ctx.wait(pe, "act", b["act"] + m - 1)
                for dc in range(NC):
                    ins = pe.matmul(psum[m % 2][:],
                                    ah[:, dc * TC + kt * 128:dc * TC + (kt + 1) * 128],
                                    xbT[:, dc * T + tc * TC:dc * T + (tc + 1) * TC],
                                    start=(dc == 0), stop=(dc == NC - 1))
                ctx.inc("pe", ins)

        b = ctx.base()

        @block.scalar
        def _(act, b=b):
            for m, (kt, tc) in enumerate(items_q):
                ctx.wait(act, "pe", b["pe"] - 16 + m + 1)
                ctx.inc("act", act.copy(QaT[:, kt * T + tc * TC:kt * T + (tc + 1) * TC],
                                        psum[m % 2][:]))

        b = ctx.base()

        @block.tensor
        def _(pe, h=h, b=b):
            ctx.wait(pe, "dve", b["dve"])
            for st in range(NC):
                ctx.wait(pe, "act", b["act"] + st - 1)
                for kt in range(4):
                    ins = pe.matmul(psum[2 + st % 2][0:128, 0:128],
                                    cT[:, kt * T + st * 128:kt * T + (st + 1) * 128],
                                    wvT[:, kt * T + h * 128:kt * T + (h + 1) * 128],
                                    start=(kt == 0), stop=(kt == 3))
                ctx.inc("pe", ins)

        b = ctx.base()

        @block.scalar
        def _(act, b=b):
            for st in range(NC):
                ctx.wait(act, "pe", b["pe"] - NC + st + 1)
                ctx.inc("act", act.copy(vh[:, st * 128:(st + 1) * 128],
                                        psum[2 + st % 2][0:128, 0:128]))

        for j in range(NT):
            smax = min(NC, 4 * (j + 1))
            minfo = []
            for i in range(smax):
                zl = i * 128 - j * TC
                minfo.append((max(zl, 0), 0 <= zl < TC))
            b = ctx.base()

            @block.tensor
            def _(pe, j=j, smax=smax, minfo=minfo, b=b):
                ctx.wait(pe, "dve", b["dve"])
                ctx.wait(pe, "act", b["act"])
                dcnt = 0
                for i in range(smax):
                    zl, hd = minfo[i]
                    ctx.wait(pe, "act", b["act"] + i - 1)
                    for kt in range(4):
                        ins = pe.matmul(psum[4 + i % 2][:],
                                        cT[:, kt * T + i * 128:kt * T + (i + 1) * 128],
                                        QaT[:, kt * T + j * TC:kt * T + (j + 1) * TC],
                                        start=(kt == 0), stop=(kt == 3))
                    ctx.inc("pe", ins)
                    nm = (1 if zl > 0 else 0) + (1 if hd else 0)
                    if nm:
                        dcnt += nm
                        ctx.wait(pe, "dve", b["dve"] + dcnt)
                    else:
                        ctx.wait(pe, "act", b["act"] + i + 1)
                    pt = sb["pt0"] if i % 2 == 0 else sb["pt1"]
                    pe.matmul(psum[6][:], vh[:, i * 128:(i + 1) * 128],
                              pt[:], start=(i == 0), stop=(i == smax - 1))
                    ctx.inc("pe", pe.matmul(psum[7][0:1, :], sb["ones_col_bf"][:],
                                            pt[:], start=(i == 0), stop=(i == smax - 1)))

            b2 = ctx.base()

            @block.scalar
            def _(act, smax=smax, b2=b2):
                for i in range(smax):
                    ctx.wait(act, "pe", b2["pe"] - 2 * smax + 2 * i + 1)
                    pt = sb["pt0"] if i % 2 == 0 else sb["pt1"]
                    ctx.inc("act", act.activation(pt[:], psum[4 + i % 2][:],
                                                  AF.Exp, scale=INVSQ))

            b3 = ctx.base()

            @block.vector
            def _(dve, smax=smax, minfo=minfo, b3=b3):
                for i in range(smax):
                    zl, hd = minfo[i]
                    if zl == 0 and not hd:
                        continue
                    ctx.wait(dve, "act", b3["act"] - smax + i + 1)
                    pt = sb["pt0"] if i % 2 == 0 else sb["pt1"]
                    if zl > 0:
                        ctx.inc("dve", dve.memset(pt[:, 0:zl], 0.0))
                    if hd:
                        ctx.inc("dve", dve.tensor_tensor(pt[:, zl:zl + 128],
                                                         pt[:, zl:zl + 128],
                                                         sb["mask_bf"][:], ALU.mult))

            b = ctx.base()

            @block.vector
            def _(dve, b=b):
                ctx.wait(dve, "pe", b["pe"])
                ctx.fence(dve, "dve", dve.tensor_scalar(sb["rows"][0:1, 0:TC],
                                                        psum[7][0:1, :], 1.0, None,
                                                        ALU.mult))
                ctx.fence(dve, "dve", dve.reciprocal(sb["rows"][0:1, TC:2 * TC],
                                                     sb["rows"][0:1, 0:TC]))
                ctx.inc("dve", dve.tensor_scalar(sb["fbc"][0:1, :],
                                                 sb["rows"][0:1, TC:2 * TC],
                                                 1.0, None, ALU.mult))

            b = ctx.base()

            @block.tensor
            def _(pe, b=b):
                ctx.wait(pe, "dve", b["dve"])
                ctx.inc("pe", pe.matmul(psum[3][:], sb["ones_row_bf"][:],
                                        sb["fbc"][0:1, :], start=True, stop=True))

            b = ctx.base()

            @block.scalar
            def _(act, b=b):
                ctx.wait(act, "pe", b["pe"])
                ctx.inc("act", act.copy(sb["fbc"][:], psum[3][:]))

            nastore = ctx.cnt["st"]
            b = ctx.base()

            @block.vector
            def _(dve, b=b):
                ctx.wait(dve, "act", b["act"])
                ctx.wait(dve, "st", b["st"])
                dve.tensor_tensor(sb["ao0"][:], psum[6][:], sb["fbc"][:], ALU.mult)
                ctx.dgate(dve, sb)

            b = ctx.base()

            @block.sync
            def _(sync, h=h, j=j, b=b):
                ctx.wait(sync, "dve", b["dve"])
                ctx.inc("st", sync.dma_start(
                    out=io["attn_bf"][h * 128:(h + 1) * 128, j * TC:(j + 1) * TC],
                    in_=sb["ao0"][:]), 16)

    def store_y(sync, j, q, ysl):
        return sync.dma_start(out=io["xT_out"][q * 128:(q + 1) * 128, j * TC:(j + 1) * TC],
                              in_=ysl)

    proj_resid_ln(ctx, block, sb, psum, io["attn_bf"], io["wo1_bf"], io["xT_in"],
                  store_y, y_bf_dst=io.get("y_bf_dst"))


def build_program_l1_attn():
    nc = bass.Bass("TRN2", target_bir_lowering=False, debug=False)
    io = {}
    io["xT_in"] = nc.dram_tensor("xT_in", [DM, T], F32, kind="ExternalInput").ap()
    io["Wd"] = nc.dram_tensor("Wd", [512, DM], F32, kind="ExternalInput").ap()
    io["Wq1"] = nc.dram_tensor("Wq1", [DM, DM], F32, kind="ExternalInput").ap()
    io["Wk1"] = nc.dram_tensor("Wk1", [DM, 512], F32, kind="ExternalInput").ap()
    io["Wv1"] = nc.dram_tensor("Wv1", [DM, 512], F32, kind="ExternalInput").ap()
    io["Wo1"] = nc.dram_tensor("Wo1", [DM, DM], F32, kind="ExternalInput").ap()
    io["xT_out"] = nc.dram_tensor("xT_out", [DM, T], F32, kind="ExternalOutput").ap()
    io["wd_bf"] = nc.dram_tensor("wd_bf", [512, DM], BF16, kind="Internal").ap()
    io["wq1_bf"] = nc.dram_tensor("wq1_bf", [DM, DM], BF16, kind="Internal").ap()
    io["wk1_bf"] = nc.dram_tensor("wk1_bf", [DM, 512], BF16, kind="Internal").ap()
    io["wv1_bf"] = nc.dram_tensor("wv1_bf", [DM, 512], BF16, kind="Internal").ap()
    io["wo1_bf"] = nc.dram_tensor("wo1_bf", [DM, DM], BF16, kind="Internal").ap()
    io["attn_bf"] = nc.dram_tensor("attn_bf", [DM, T], BF16, kind="Internal").ap()
    cap = declare_consts(nc)
    sb, psum = alloc_common(nc)
    ctx = Ctx(nc)
    with nc.Block() as block:
        load_consts(ctx, block, cap, sb)
        build_l1_attn(ctx, block, io, sb, psum)
        finish(ctx, block)
    return nc


def _run_l1_attn(xT_all, weights):
    nc = build_program_l1_attn()
    consts = host_constants()
    in_maps = []
    for cid in range(8):
        m = {"xT_in": np.ascontiguousarray(xT_all[cid]),
             "Wd": weights["L1_attn_Wd"], "Wq1": weights["L1_attn_Wq"],
             "Wk1": weights["L1_attn_Wk"], "Wv1": weights["L1_attn_Wv"],
             "Wo1": weights["L1_attn_Wo"]}
        for k, v in consts.items():
            m["c_" + k] = v
        in_maps.append(m)
    res = run_bass_kernel_spmd(nc, in_maps, core_ids=list(range(8)))
    return np.stack([res.results[i]["xT_out"] for i in range(8)])



def lut_decode_unit(ctx, block, sb, idx_dram, dec_dram, it, width,
                    bufs=None):
    if bufs is None:
        idx_u16 = sb["prhs"][:].bitcast(U16)
        acc = sb["un1"][:, 0:width]
        tmp = sb["un1"][:, width:2 * width]
    else:
        idx_u16, acc, tmp = bufs
    b = ctx.base()

    @block.gpsimd
    def _(g, it=it, b=b):
        ctx.wait(g, "dve", b["dve"])
        ctx.inc("wdma", g.dma_start(out=idx_u16[:, 0:width],
                                    in_=idx_dram[it * 128:(it + 1) * 128, :]), 16)
        ctx.inc("wdma", g.dma_start(out=sb["dly"][0:1, 0:32],
                                    in_=idx_dram[0:1, 0:32]), 16)

    b = ctx.base()

    @block.vector
    def _(dve, it=it, b=b):
        ctx.wait(dve, "wdma", b["wdma"])
        ctx.wait(dve, "act", b["act"])
        ctx.wait(dve, "st", b["st"])
        dve.tensor_scalar(acc, idx_u16[:, 0:width], 0.0, sb["lutbc"][:, 0:1],
                          ALU.is_equal, ALU.mult)
        for j in range(1, 16):
            dve.tensor_scalar(tmp, idx_u16[:, 0:width], float(j),
                              sb["lutbc"][:, j:j + 1], ALU.is_equal, ALU.mult)
            dve.tensor_tensor(acc, acc, tmp, ALU.add)
        ctx.dgate(dve, sb)

    b = ctx.base()

    @block.sync
    def _(sync, it=it, b=b):
        ctx.wait(sync, "dve", b["dve"])
        ctx.inc("st", sync.dma_start(out=dec_dram[it * 128:(it + 1) * 128, :],
                                     in_=acc), 16)


def lut_decode_tiles(ctx, block, io, sb, idx_dram, dec_dram, nrows, width):
    for it in range(nrows // 128):
        lut_decode_unit(ctx, block, sb, idx_dram, dec_dram, it, width)


def prep_lutbc(ctx, block, io, sb, psum):
    b = ctx.base()

    @block.sync
    def _(sync, b=b):
        ctx.inc("ld", sync.dma_start(out=sb["rows"][0:1, 0:16], in_=io["lut"][:, :]), 16)

    b = ctx.base()

    @block.tensor
    def _(pe, b=b):
        ctx.wait(pe, "ld", b["ld"])
        ctx.inc("pe", pe.matmul(psum[0][0:128, 0:16], sb["ones_row_f32"][:],
                                sb["rows"][0:1, 0:16], start=True, stop=True))

    b = ctx.base()

    @block.scalar
    def _(act, b=b):
        ctx.wait(act, "pe", b["pe"])
        ctx.inc("act", act.copy(sb["lutbc"][:], psum[0][0:128, 0:16]))


def stage_wdT(ctx, block, io, sb, dec_dram, wdT_dram, ninter):
    noc = ninter // 128
    for oc in range(noc):
        b = ctx.base()

        @block.sync
        def _(sync, oc=oc, b=b):
            ctx.wait(sync, "st", b["st"])
            ctx.inc("ld", sync.dma_start(out=sb["wslab0"][:, 0:DM],
                                         in_=dec_dram[0:DM, oc * 128:(oc + 1) * 128],
                                         transpose=True), 16)

        b = ctx.base()

        @block.sync
        def _(sync, oc=oc, b=b):
            ctx.wait(sync, "ld", b["ld"])
            ctx.inc("st", sync.dma_start(out=wdT_dram[oc * 128:(oc + 1) * 128, :],
                                         in_=sb["wslab0"][:, 0:DM]), 16)


def ffn_gateup(ctx, block, io, sb, psum, gdec, udec, inter_dram, ninter, kind, filler=None):
    import os as _os
    _maxslab = int(_os.environ.get("FFN_MAXSLAB", "9999"))
    noc = ninter // 128
    slabs = []
    o = 0
    while o < noc:
        w = min(4, noc - o)
        slabs.append((o, w))
        o += w
    A = 0.7978845608028654
    B = 0.044715
    gu_mark = [0]
    for (obase, nsub) in slabs[:_maxslab]:
        sw = nsub * 128
        b = ctx.base()

        @block.sync
        def _(sync, obase=obase, nsub=nsub, sw=sw, b=b):
            ctx.wait(sync, "pe", b["pe"])
            ctx.wait(sync, "st", b["st"])
            for c in range(NC):
                ctx.inc("ld", sync.dma_start(
                    out=sb["wslab0"][:, c * TC:c * TC + sw],
                    in_=gdec[obase * 128:obase * 128 + sw, c * 128:(c + 1) * 128],
                    transpose=True), 16)
                ctx.inc("ld", sync.dma_start(
                    out=sb["wslab1"][:, c * TC:c * TC + sw],
                    in_=udec[obase * 128:obase * 128 + sw, c * 128:(c + 1) * 128],
                    transpose=True), 16)

        items = [(ss, tc) for ss in range(nsub) for tc in range(NT)]
        npd = 1 if kind == "silu" else 2
        b = ctx.base()
        pm = gu_mark[0]

        @block.tensor
        def _(pe, obase=obase, items=items, b=b, npd=npd, pm=pm):
            ctx.wait(pe, "ld", b["ld"])
            ctx.wait(pe, "wdma", b["wdma"])
            for m, (ss, tc) in enumerate(items):
                if m < 2:
                    ctx.wait(pe, "dve", pm if pm else b["dve"] + npd * (m - 1))
                else:
                    ctx.wait(pe, "dve", b["dve"] + npd * (m - 1))
                for c in range(NC):
                    i1 = pe.matmul(psum[2 * (m % 2)][:],
                                   sb["wslab0"][:, c * TC + ss * 128:c * TC + (ss + 1) * 128],
                                   sb["xbT"][:, c * T + tc * TC:c * T + (tc + 1) * TC],
                                   start=(c == 0), stop=(c == NC - 1))
                ctx.inc("pe", i1)
                for c in range(NC):
                    i2 = pe.matmul(psum[2 * (m % 2) + 1][:],
                                   sb["wslab1"][:, c * TC + ss * 128:c * TC + (ss + 1) * 128],
                                   sb["xbT"][:, c * T + tc * TC:c * T + (tc + 1) * TC],
                                   start=(c == 0), stop=(c == NC - 1))
                ctx.inc("pe", i2)

        b2 = ctx.base()

        @block.scalar
        def _(act, items=items, kind=kind, b2=b2):
            for m, (ss, tc) in enumerate(items):
                pg = psum[2 * (m % 2)]
                if kind == "silu":
                    ctx.wait(act, "pe", b2["pe"] - 2 * len(items) + 2 * m + 1)
                    ctx.wait(act, "dve", b2["dve"] + m - 1)
                    ctx.inc("act", act.activation((sb["sc1"] if m % 2 == 0 else sb["sc2"])[:],
                                                  pg[:], AF.Silu))
                else:
                    ctx.wait(act, "pe", b2["pe"] - 2 * len(items) + 2 * m + 1)
                    ctx.wait(act, "dve", b2["dve"] + 2 * (m - 1))
                    ctx.inc("act", act.activation((sb["sc1"] if m % 2 == 0 else sb["sc2"])[:],
                                                  pg[:], AF.Square))
                    ctx.wait(act, "dve", b2["dve"] + 2 * m + 1)
                    ctx.inc("act", act.activation((sb["sc1"] if m % 2 == 0 else sb["sc2"])[:],
                                                  (sb["sc1"] if m % 2 == 0 else sb["sc2"])[:],
                                                  AF.Tanh))

        b3 = ctx.base()

        @block.vector
        def _(dve, items=items, kind=kind, b3=b3, obase=obase):
            for m, (ss, tc) in enumerate(items):
                pg = psum[2 * (m % 2)]
                pu = psum[2 * (m % 2) + 1]
                sc = sb["sc1"] if m % 2 == 0 else sb["sc2"]
                ao = sb["ao0"] if m % 2 == 0 else sb["ao1"]
                if kind == "silu":
                    ctx.wait(dve, "act", b3["act"] - len(items) + m + 1)
                    ctx.wait(dve, "pe", b3["pe"] - 2 * len(items) + 2 * m + 2)
                    ctx.wait(dve, "st", b3["st"] + (m - 1) * 16)
                    dve.tensor_tensor(ao[:], sc[:], pu[:], ALU.mult)
                    ctx.dgate(dve, sb)
                else:
                    ctx.wait(dve, "act", b3["act"] - 2 * len(items) + 2 * m + 1)
                    ctx.wait(dve, "pe", b3["pe"] - 2 * len(items) + 2 * m + 2)
                    dve.tensor_scalar(sc[:], sc[:], A * B, A, ALU.mult, ALU.add)
                    ctx.inc("dve", dve.tensor_tensor(sc[:], sc[:], pg[:], ALU.mult))
                    ctx.wait(dve, "act", b3["act"] - 2 * len(items) + 2 * m + 2)
                    ctx.wait(dve, "st", b3["st"] + (m - 1) * 16)
                    dve.tensor_scalar(sc[:], sc[:], 0.5, 0.5, ALU.mult, ALU.add)
                    dve.tensor_tensor(sc[:], sc[:], pg[:], ALU.mult)
                    dve.tensor_tensor(ao[:], sc[:], pu[:], ALU.mult)
                    ctx.dgate(dve, sb)

        gu_mark[0] = ctx.cnt["dve"]
        b4 = ctx.base()
        npq = 2 if kind != "silu" else 1

        @block.sync
        def _(sync, items=items, obase=obase, b4=b4, npq=npq):
            for m, (ss, tc) in enumerate(items):
                ao = sb["ao0"] if m % 2 == 0 else sb["ao1"]
                ctx.wait(sync, "dve", b4["dve"] - npq * len(items) + npq * m + npq)
                orow = (obase + ss) * 128
                ctx.inc("st", sync.dma_start(
                    out=inter_dram[orow:orow + 128, tc * TC:(tc + 1) * TC],
                    in_=ao[:]), 16)
        if filler is not None:
            filler.run(8)

    if filler is not None:
        filler.drain()


def ffn_down_ln(ctx, block, io, sb, psum, wdT_dram, inter_dram, ninter, xT_src,
                store_y, y_bf_dst=None):
    import os as _os
    _maxj = int(_os.environ.get("FFN_MAXJ", "4"))
    _maxoc = int(_os.environ.get("FFN_MAXOC", "9999"))
    _dpart = int(_os.environ.get("FFN_DOWNPART", "4"))
    noc = min(ninter // 128, _maxoc)
    wdt0 = sb["wslab0"][:]
    it0 = sb["prhs"][:]
    for j in range(_maxj):
        for half in range(2):
            pe_base = ctx.cnt["pe"]
            b = ctx.base()

            @block.sync
            def _(sync, j=j, half=half, b=b, pe_base=pe_base):
                ctx.wait(sync, "pe", pe_base)
                ctx.wait(sync, "st", b["st"])
                for oc in range(noc):
                    if oc >= 4:
                        ctx.wait(sync, "pe", pe_base + oc - 3)
                    ctx.inc("ld", sync.dma_start(
                        out=wdt0[:, (oc % 4) * 1024:(oc % 4 + 1) * 1024],
                        in_=wdT_dram[half * 1024:(half + 1) * 1024,
                                     oc * 128:(oc + 1) * 128],
                        transpose=True), 16)
                    ctx.inc("ld", sync.dma_start(
                        out=it0[:, (oc % 4) * TC:(oc % 4 + 1) * TC],
                        in_=inter_dram[oc * 128:(oc + 1) * 128,
                                       j * TC:(j + 1) * TC]), 16)
                ctx.inc("ld", sync.dma_start(out=sb["dly"][0:1, 0:64],
                                             in_=xT_src[0:1, 0:64]), 16)

            b2 = ctx.base()

            @block.tensor
            def _(pe, j=j, half=half, b2=b2):
                ctx.wait(pe, "dve", b2["dve"])
                ctx.wait(pe, "act", b2["act"])
                ctx.wait(pe, "st", b2["st"])
                total = 32 * noc + 16
                for oc in range(noc):
                    ctx.wait(pe, "ld", min(b2["ld"], b2["ld"] - total + 32 * (oc + 2)))
                    for u in range(8):
                        ins = pe.matmul(psum[(u + 2) % 8][:],
                                        wdt0[:, (oc % 4) * 1024 + u * 128:(oc % 4) * 1024 + (u + 1) * 128],
                                        it0[:, (oc % 4) * TC:(oc % 4 + 1) * TC],
                                        start=(oc == 0), stop=(oc == noc - 1))
                    ctx.inc("pe", ins)

            if _dpart <= 1:
                continue
            for w in range(2):
                b3 = ctx.base()

                @block.sync
                def _(sync, j=j, half=half, w=w, b3=b3):
                    ctx.wait(sync, "dve", b3["dve"])
                    for uu in range(4):
                        u = w * 4 + uu
                        q = half * 8 + u
                        ctx.inc("ld", sync.dma_start(
                            out=sb["xt_ring"][:, uu * TC:(uu + 1) * TC],
                            in_=xT_src[q * 128:(q + 1) * 128, j * TC:(j + 1) * TC]), 16)
                    ctx.inc("ld", sync.dma_start(out=sb["dly"][0:1, 0:32],
                                                 in_=xT_src[0:1, 0:32]), 16)

                b4 = ctx.base()

                @block.vector
                def _(dve, j=j, half=half, w=w, b4=b4):
                    ctx.wait(dve, "ld", b4["ld"])
                    ctx.wait(dve, "pe", b4["pe"])
                    for uu in range(4):
                        u = w * 4 + uu
                        q = half * 8 + u
                        ctx.inc("dve", dve.tensor_tensor(
                            sb["resid"][:, q * TC:(q + 1) * TC], psum[(u + 2) % 8][:],
                            sb["xt_ring"][:, uu * TC:(uu + 1) * TC], ALU.add))

        if _dpart <= 2:
            continue
        b5 = ctx.base()
        pe_stats_base = ctx.cnt["pe"]

        @block.scalar
        def _(act, j=j, b5=b5, pe_stats_base=pe_stats_base):
            for q in range(NC):
                ctx.wait(act, "dve", b5["dve"] - 16 + q + 1)
                if q >= 2:
                    ctx.wait(act, "pe", pe_stats_base + (q - 2) + 1)
                ctx.inc("act", act.activation(
                    (sb["sc1"] if q % 2 == 0 else sb["sc2"])[:],
                    sb["resid"][:, q * TC:(q + 1) * TC], AF.Square))

        b6 = ctx.base()

        @block.tensor
        def _(pe, j=j, b6=b6):
            ctx.wait(pe, "dve", b6["dve"])
            for q in range(NC):
                ctx.wait(pe, "act", b6["act"] - 16 + q + 1)
                i1 = pe.matmul(psum[6][0:1, :], sb["ones_col_f32"][:],
                               sb["resid"][:, q * TC:(q + 1) * TC],
                               start=(q == 0), stop=(q == NC - 1))
                i2 = pe.matmul(psum[7][0:1, :], sb["ones_col_f32"][:],
                               (sb["sc1"] if q % 2 == 0 else sb["sc2"])[:],
                               start=(q == 0), stop=(q == NC - 1))
                ctx.inc("pe", i2)

        rows = sb["rows"]
        b = ctx.base()

        @block.vector
        def _(dve, j=j, b=b):
            ctx.wait(dve, "pe", b["pe"])
            ctx.fence(dve, "dve", dve.tensor_scalar(rows[0:1, 0:TC], psum[6][0:1, :],
                                                    1.0 / DM, None, ALU.mult))
            ctx.fence(dve, "dve", dve.tensor_scalar(rows[0:1, TC:2 * TC],
                                                    psum[7][0:1, :], 1.0 / DM, None,
                                                    ALU.mult))
            ctx.fence(dve, "dve", dve.tensor_tensor(rows[0:1, 2 * TC:3 * TC],
                                                    rows[0:1, 0:TC], rows[0:1, 0:TC],
                                                    ALU.mult))
            ctx.inc("dve", dve.tensor_tensor(rows[0:1, TC:2 * TC],
                                             rows[0:1, TC:2 * TC],
                                             rows[0:1, 2 * TC:3 * TC], ALU.subtract))

        b = ctx.base()

        @block.scalar
        def _(act, j=j, b=b):
            ctx.wait(act, "dve", b["dve"])
            ctx.inc("act", act.activation(rows[0:1, 2 * TC:3 * TC],
                                          rows[0:1, TC:2 * TC], AF.Sqrt,
                                          bias=sb["eps_f32"][0:1, 0:1]))

        b = ctx.base()

        @block.vector
        def _(dve, j=j, b=b):
            ctx.wait(dve, "act", b["act"])
            ctx.fence(dve, "dve", dve.reciprocal(rows[0:1, TC:2 * TC],
                                                 rows[0:1, 2 * TC:3 * TC]))
            ctx.inc("dve", dve.tensor_tensor(rows[0:1, 2 * TC:3 * TC], rows[0:1, 0:TC],
                                             rows[0:1, TC:2 * TC], ALU.mult))

        b = ctx.base()

        @block.tensor
        def _(pe, j=j, b=b):
            ctx.wait(pe, "dve", b["dve"])
            ctx.inc("pe", pe.matmul(psum[4][:], sb["ones_row_f32"][:],
                                    rows[0:1, TC:2 * TC], start=True, stop=True))
            ctx.inc("pe", pe.matmul(psum[5][:], sb["ones_row_f32"][:],
                                    rows[0:1, 2 * TC:3 * TC], start=True, stop=True))

        b = ctx.base()

        @block.scalar
        def _(act, j=j, b=b):
            ctx.wait(act, "pe", b["pe"])
            act.copy(sb["rstd_bc"][:], psum[4][:])
            ctx.inc("act", act.copy(sb["mrow_bc"][:], psum[5][:]))

        if _dpart <= 3:
            continue
        b = ctx.base()
        npq = 2 if y_bf_dst is not None else 1

        @block.vector
        def _(dve, j=j, b=b, npq=npq):
            ctx.wait(dve, "act", b["act"])
            for q in range(NC):
                slot = q % 2
                if q >= 2:
                    ctx.wait(dve, "st", b["st"] + (q - 1) * 16)
                dve.tensor_tensor(sb["sc1"][:], sb["resid"][:, q * TC:(q + 1) * TC],
                                  sb["rstd_bc"][:], ALU.mult)
                dve.tensor_tensor(
                    sb["y_ring"][:, slot * TC:(slot + 1) * TC],
                    sb["sc1"][:], sb["mrow_bc"][:], ALU.subtract)
                if y_bf_dst is not None:
                    dve.tensor_copy(
                        y_bf_dst[:, q * T + j * TC:q * T + (j + 1) * TC],
                        sb["y_ring"][:, slot * TC:(slot + 1) * TC])
                ctx.dgate(dve, sb)

        b = ctx.base()

        @block.sync
        def _(sync, j=j, b=b, npq=npq):
            for q in range(NC):
                slot = q % 2
                ctx.wait(sync, "dve", b["dve"] - NC + q + 1)
                ctx.inc("st", store_y(sync, j, q,
                                      sb["y_ring"][:, slot * TC:(slot + 1) * TC]), 16)


def build_l1_ffn(ctx, block, io, sb, psum, stage=9, skip_decode=False):
    @block.gpsimd
    def _(g):
        for c in range(NC):
            ctx.inc("wdma", g.dma_start(out=sb["xbT"][:, c * T:(c + 1) * T],
                                        in_=io["xT_in"][c * 128:(c + 1) * 128, :]), 16)

    if not skip_decode:
        prep_lutbc(ctx, block, io, sb, psum)
        lut_decode_tiles(ctx, block, io, sb, io["gate_idx"], io["g1_dec"], 5504, DM)
        if stage <= 1:
            return
        lut_decode_tiles(ctx, block, io, sb, io["up_idx"], io["u1_dec"], 5504, DM)
        lut_decode_tiles(ctx, block, io, sb, io["down_idx"], io["d1_dec"], DM, 5504)
        if stage <= 2:
            return
    ffn_gateup(ctx, block, io, sb, psum, io["g1_dec"], io["u1_dec"],
               io["inter_dram"], 5504, "silu")
    if stage <= 3:
        return

    def store_y(sync, j, q, ysl):
        return sync.dma_start(out=io["xT_out"][q * 128:(q + 1) * 128, j * TC:(j + 1) * TC],
                              in_=ysl)

    ffn_down_ln(ctx, block, io, sb, psum, io["d1_dec"], io["inter_dram"], 5504,
                io["xT_in"], store_y, y_bf_dst=io.get("y_bf_dst"))


def build_program_l1_ffn(stage=9):
    nc = bass.Bass("TRN2", target_bir_lowering=False, debug=False)
    io = {}
    io["xT_in"] = nc.dram_tensor("xT_in", [DM, T], F32, kind="ExternalInput").ap()
    io["lut"] = nc.dram_tensor("lut", [1, 16], F32, kind="ExternalInput").ap()
    io["gate_idx"] = nc.dram_tensor("gate_idx", [5504, DM], I32, kind="ExternalInput").ap()
    io["up_idx"] = nc.dram_tensor("up_idx", [5504, DM], I32, kind="ExternalInput").ap()
    io["down_idx"] = nc.dram_tensor("down_idx", [DM, 5504], I32, kind="ExternalInput").ap()
    io["xT_out"] = nc.dram_tensor("xT_out", [DM, T], F32, kind="ExternalOutput").ap()
    io["g1_dec"] = nc.dram_tensor("g1_dec", [5504, DM], BF16,
                                  kind="ExternalOutput" if stage == 1 else "Internal").ap()
    io["u1_dec"] = nc.dram_tensor("u1_dec", [5504, DM], BF16, kind="Internal").ap()
    io["d1_dec"] = nc.dram_tensor("d1_dec", [DM, 5504], BF16, kind="Internal").ap()
    io["wdT1_dram"] = nc.dram_tensor("wdT1_dram", [5504, DM], BF16,
                                     kind="ExternalOutput" if stage == 2 else "Internal").ap()
    io["inter_dram"] = nc.dram_tensor("inter_dram", [5504, T], BF16,
                                      kind="ExternalOutput" if stage == 3 else "Internal").ap()
    cap = declare_consts(nc)
    sb, psum = alloc_common(nc)
    ctx = Ctx(nc)
    with nc.Block() as block:
        load_consts(ctx, block, cap, sb)
        build_l1_ffn(ctx, block, io, sb, psum, stage=stage)
        finish(ctx, block)
    return nc


def _run_l1_ffn(xT_all, weights, stage=9):
    nc = build_program_l1_ffn(stage=stage)
    consts = host_constants()
    in_maps = []
    for cid in range(8):
        m = {"xT_in": np.ascontiguousarray(xT_all[cid]),
             "lut": np.ascontiguousarray(weights["L1_ffn_lut"].reshape(1, 16)),
             "gate_idx": weights["L1_ffn_gate_idx"], "up_idx": weights["L1_ffn_up_idx"],
             "down_idx": weights["L1_ffn_down_idx"]}
        for k, v in consts.items():
            m["c_" + k] = v
        in_maps.append(m)
    res = run_bass_kernel_spmd(nc, in_maps, core_ids=list(range(8)))
    key = {1: "g1_dec", 2: "wdT1_dram", 3: "inter_dram"}.get(stage, "xT_out")
    return np.stack([res.results[i][key] for i in range(8)])



def nvfp4_scl_unit(ctx, block, sb, sc_dram, it, nch):
    bsc = ctx.base()

    @block.sync
    def _(sync, it=it, bsc=bsc):
        ctx.wait(sync, "dve", bsc["dve"])
        ctx.inc("ld", sync.dma_start(out=sb["scl"][:, 0:64 * nch],
                                     in_=sc_dram[it * 128:(it + 1) * 128, :]), 16)
        ctx.inc("ld", sync.dma_start(out=sb["dly"][0:1, 0:32],
                                     in_=sc_dram[0:1, 0:32]), 16)


def nvfp4_chunk_unit(ctx, block, sb, pk_dram, dec_dram, it, ch, nch, alt=False):
    pv = sb["prhs"][:].bitcast(U16)
    bv = sb["prhs"][:]
    pk = pv[:, 0:1024]
    bs4 = pv[:, 1024:2048]
    m_lo = pv[:, 2048:3072]
    m_hi = pv[:, 3072:4096]
    g_lo = bv[:, 4096:5120]
    g_hi = bv[:, 5120:6144]
    t1 = bv[:, 6144:7168]
    d0 = bv[:, 7168:8192]
    if alt:
        t2 = sb["y_ring"][:].bitcast(BF16)[:, 0:1024]
        wout = sb["xt_ring"][:].bitcast(BF16)[:, 0:2048]
    else:
        t2 = sb["un1"][:, 2048:3072]
        wout = sb["un1"][:, 0:2048]
    b = ctx.base()

    @block.gpsimd
    def _(g, it=it, ch=ch, b=b):
        ctx.wait(g, "dve", b["dve"])
        ctx.inc("wdma", g.dma_start(
            out=pk,
            in_=pk_dram[it * 128:(it + 1) * 128, ch * 1024:(ch + 1) * 1024]), 16)
        ctx.inc("wdma", g.dma_start(out=sb["dly"][0:1, 32:64].bitcast(U16),
                                    in_=pk_dram[0:1, 0:64]), 16)

    b = ctx.base()

    @block.vector
    def _(dve, it=it, ch=ch, b=b, nch=nch):
        ctx.wait(dve, "wdma", b["wdma"])
        ctx.wait(dve, "ld", b["ld"])
        ctx.wait(dve, "st", b["st"])
        if ch == 0:
            ctx.fence(dve, "dve", dve.tensor_scalar(
                sb["scl"][:, 0:64 * nch], sb["scl"][:, 0:64 * nch],
                -0.125, None, ALU.mult))
        sexp = sb["scl"][:, ch * 64:(ch + 1) * 64].unsqueeze(2) \
            .broadcast_to([128, 64, 16])
        dve.tensor_scalar(bs4, pk, 4, None, ALU.logical_shift_right)
        dve.tensor_single_scalar(m_lo, pk, 7, ALU.bitwise_and)
        dve.tensor_single_scalar(m_hi, bs4, 7, ALU.bitwise_and)
        gu_lo = g_lo.bitcast(U16)
        gu_hi = g_hi.bitcast(U16)
        dve.tensor_single_scalar(gu_lo, pk, 8, ALU.bitwise_and)
        dve.tensor_single_scalar(gu_hi, bs4, 8, ALU.bitwise_and)
        dve.tensor_scalar(g_lo, gu_lo, 1.0, -4.0, ALU.mult, ALU.add)
        dve.tensor_scalar(g_hi, gu_hi, 1.0, -4.0, ALU.mult, ALU.add)
        for (mm, gg, off) in ((m_lo, g_lo, 0), (m_hi, g_hi, 1)):
            dve.tensor_scalar(t1, mm, 2.0, -4.0, ALU.mult, ALU.add)
            dve.tensor_tensor(t2, mm, t1, ALU.max)
            dve.tensor_scalar(t1, mm, 4.0, -16.0, ALU.mult, ALU.add)
            dve.tensor_tensor(t2, t2, t1, ALU.max)
            dve.tensor_tensor(d0, t2, gg, ALU.mult)
            dve.tensor_tensor(
                wout[:, off:2048:2].rearrange("p (a b) -> p a b", b=16),
                d0[:, :].rearrange("p (a b) -> p a b", b=16),
                sexp, ALU.mult)
        ctx.dgate(dve, sb)

    b = ctx.base()

    @block.sync
    def _(sync, it=it, ch=ch, b=b):
        ctx.wait(sync, "dve", b["dve"])
        ctx.inc("st", sync.dma_start(
            out=dec_dram[it * 128:(it + 1) * 128, ch * 2048:(ch + 1) * 2048],
            in_=wout), 16)


def nvfp4_decode_tiles(ctx, block, io, sb, pk_dram, sc_dram, dec_dram, nrows, kbytes,
                       alt=False):
    ntile = nrows // 128
    nch = kbytes // 1024
    for it in range(ntile):
        nvfp4_scl_unit(ctx, block, sb, sc_dram, it, nch)
        for ch in range(nch):
            nvfp4_chunk_unit(ctx, block, sb, pk_dram, dec_dram, it, ch, nch, alt=alt)


def build_l0_ffn(ctx, block, io, sb, psum, stage=9, skip_decode=False, filler=None):
    @block.gpsimd
    def _(g):
        for c in range(NC):
            ctx.inc("wdma", g.dma_start(out=sb["xbT"][:, c * T:(c + 1) * T],
                                        in_=io["xT_in"][c * 128:(c + 1) * 128, :]), 16)

    if not skip_decode:
        nvfp4_decode_tiles(ctx, block, io, sb, io["gate_W_packed"], io["gate_scales_n"],
                           io["g0_dec"], 16384, 1024)
        if stage <= 1:
            return
        nvfp4_decode_tiles(ctx, block, io, sb, io["up_W_packed"], io["up_scales_n"],
                           io["u0_dec"], 16384, 1024)
        nvfp4_decode_tiles(ctx, block, io, sb, io["down_W_packed"], io["down_scales_n"],
                           io["d0_dec"], 2048, 8192)
        if stage <= 2:
            return
    ffn_gateup(ctx, block, io, sb, psum, io["g0_dec"], io["u0_dec"],
               io["inter0_dram"], 16384, "gelu", filler=filler)
    if stage <= 3:
        return

    def store_y(sync, j, q, ysl):
        return sync.dma_start(out=io["xT_out"][q * 128:(q + 1) * 128, j * TC:(j + 1) * TC],
                              in_=ysl)

    ffn_down_ln(ctx, block, io, sb, psum, io["d0_dec"], io["inter0_dram"], 16384,
                io["xT_in"], store_y, y_bf_dst=io.get("y_bf_dst"))


def build_program_l0_ffn(stage=9):
    nc = bass.Bass("TRN2", target_bir_lowering=False, debug=False)
    io = {}
    io["xT_in"] = nc.dram_tensor("xT_in", [DM, T], F32, kind="ExternalInput").ap()
    io["gate_W_packed"] = nc.dram_tensor("gate_W_packed", [16384, 1024], I32,
                                         kind="ExternalInput").ap()
    io["gate_scales_n"] = nc.dram_tensor("gate_scales_n", [16384, 64], F32,
                                         kind="ExternalInput").ap()
    io["up_W_packed"] = nc.dram_tensor("up_W_packed", [16384, 1024], I32,
                                       kind="ExternalInput").ap()
    io["up_scales_n"] = nc.dram_tensor("up_scales_n", [16384, 64], F32,
                                       kind="ExternalInput").ap()
    io["down_W_packed"] = nc.dram_tensor("down_W_packed", [2048, 8192], I32,
                                         kind="ExternalInput").ap()
    io["down_scales_n"] = nc.dram_tensor("down_scales_n", [2048, 512], F32,
                                         kind="ExternalInput").ap()
    io["xT_out"] = nc.dram_tensor("xT_out", [DM, T], F32, kind="ExternalOutput").ap()
    io["g0_dec"] = nc.dram_tensor("g0_dec", [16384, DM], BF16,
                                  kind="ExternalOutput" if stage == 1 else "Internal").ap()
    io["u0_dec"] = nc.dram_tensor("u0_dec", [16384, DM], BF16, kind="Internal").ap()
    io["d0_dec"] = nc.dram_tensor("d0_dec", [DM, 16384], BF16, kind="Internal").ap()
    io["wdT0_dram"] = nc.dram_tensor("wdT0_dram", [16384, DM], BF16,
                                     kind="ExternalOutput" if stage == 2 else "Internal").ap()
    io["inter0_dram"] = nc.dram_tensor("inter0_dram", [16384, T], BF16,
                                       kind="ExternalOutput" if stage == 3 else "Internal").ap()
    cap = declare_consts(nc)
    sb, psum = alloc_common(nc)
    ctx = Ctx(nc)
    with nc.Block() as block:
        load_consts(ctx, block, cap, sb)
        build_l0_ffn(ctx, block, io, sb, psum, stage=stage)
        finish(ctx, block)
    return nc


def _run_l0_ffn(xT_all, weights, stage=9):
    nc = build_program_l0_ffn(stage=stage)
    consts = host_constants()
    in_maps = []
    for cid in range(8):
        m = {"xT_in": np.ascontiguousarray(xT_all[cid]),
             "gate_W_packed": weights["L0_ffn_gate_W_packed"],
             "gate_scales_n": weights["L0_ffn_gate_scales"],
             "up_W_packed": weights["L0_ffn_up_W_packed"],
             "up_scales_n": weights["L0_ffn_up_scales"],
             "down_W_packed": weights["L0_ffn_down_W_packed"],
             "down_scales_n": weights["L0_ffn_down_scales"]}
        for k, v in consts.items():
            m["c_" + k] = v
        in_maps.append(m)
    res = run_bass_kernel_spmd(nc, in_maps, core_ids=list(range(8)))
    key = {1: "g0_dec", 2: "wdT0_dram", 3: "inter0_dram"}.get(stage, "xT_out")
    return np.stack([res.results[i][key] for i in range(8)])




def barrier_all(ctx, block):
    c = ctx.base()
    for name in ("sync", "tensor", "vector", "scalar", "gpsimd"):
        def body(eng, c=c):
            for s in ("ld", "wdma", "st", "pe", "act", "dve"):
                ctx.wait(eng, s, c[s])
        getattr(block, name)(body)


def build_program_fused(dbg_inter=False):
    nc = bass.Bass("TRN2", target_bir_lowering=False, debug=False)
    io = {}
    io["x0T"] = nc.dram_tensor("x0T", [DM, T], F32, kind="ExternalInput").ap()
    for n, shp in (("Wq", [DM, DM]), ("Wk", [512, DM]), ("Wv", [512, DM]),
                   ("Wo", [DM, DM]), ("Wgslc", [16, DM]), ("Wgswa", [16, DM]),
                   ("Wd", [512, DM]), ("Wq1", [DM, DM]), ("Wk1", [DM, 512]),
                   ("Wv1", [DM, 512]), ("Wo1", [DM, DM]), ("lut", [1, 16])):
        io[n] = nc.dram_tensor(n, shp, F32, kind="ExternalInput").ap()
    for n, shp in (("gate_W_packed", [16384, 1024]), ("up_W_packed", [16384, 1024]),
                   ("down_W_packed", [2048, 8192]), ("gate_idx", [5504, DM]),
                   ("up_idx", [5504, DM]), ("down_idx", [DM, 5504])):
        io[n] = nc.dram_tensor(n, shp, I32, kind="ExternalInput").ap()
    for n, shp in (("gate_scales_n", [16384, 64]), ("up_scales_n", [16384, 64]),
                   ("down_scales_n", [2048, 512])):
        io[n] = nc.dram_tensor(n, shp, F32, kind="ExternalInput").ap()
    io["yT"] = nc.dram_tensor("yT", [DM, T], F32, kind="ExternalOutput").ap()
    ik = "ExternalOutput" if dbg_inter else "Internal"
    io["x1T"] = nc.dram_tensor("x1T", [DM, T], F32, kind=ik).ap()
    io["x2T"] = nc.dram_tensor("x2T", [DM, T], F32, kind=ik).ap()
    io["x3T"] = nc.dram_tensor("x3T", [DM, T], F32, kind=ik).ap()
    for n, shp in (("wq_bf", [DM, DM]), ("wk_bf", [512, DM]), ("wv_bf", [512, DM]),
                   ("wo_bf", [DM, DM]), ("wgs_bf", [32, DM]), ("attn_bf", [DM, T]),
                   ("wd_bf", [512, DM]), ("wq1_bf", [DM, DM]), ("wk1_bf", [DM, 512]),
                   ("wv1_bf", [DM, 512]), ("wo1_bf", [DM, DM]),
                   ("g0_dec", [16384, DM]), ("u0_dec", [16384, DM]),
                   ("d0_dec", [DM, 16384]), ("wdT0_dram", [16384, DM]),
                   ("inter0_dram", [16384, T]),
                   ("g1_dec", [5504, DM]), ("u1_dec", [5504, DM]),
                   ("d1_dec", [DM, 5504]), ("wdT1_dram", [5504, DM]),
                   ("inter_dram", [5504, T])):
        io[n] = nc.dram_tensor(n, shp, BF16, kind="Internal").ap()
    cap = declare_consts(nc)
    sb, psum = alloc_common(nc)
    ctx = Ctx(nc)
    with nc.Block() as block:
        load_consts(ctx, block, cap, sb)
        prep_lutbc(ctx, block, io, sb, psum)

        fa = Fillers()
        for mat, scm, dec in (("gate_W_packed", "gate_scales_n", "g0_dec"),
                              ("up_W_packed", "up_scales_n", "u0_dec")):
            for it in range(128):
                def unit(it=it, mat=mat, scm=scm, dec=dec):
                    nvfp4_scl_unit(ctx, block, sb, io[scm], it, 1)
                    nvfp4_chunk_unit(ctx, block, sb, io[mat], io[dec], it, 0, 1,
                                     alt=True)
                fa.q.append(unit)
        fg = Fillers()
        for it in range(16):
            def unit_s(it=it):
                nvfp4_scl_unit(ctx, block, sb, io["down_scales_n"], it, 8)
            fg.q.append(unit_s)
            for ch in range(8):
                def unit_c(it=it, ch=ch):
                    nvfp4_chunk_unit(ctx, block, sb, io["down_W_packed"],
                                     io["d0_dec"], it, ch, 8, alt=False)
                fg.q.append(unit_c)
        for mat, dec, nrows, width in (("gate_idx", "g1_dec", 5504, DM),
                                       ("up_idx", "u1_dec", 5504, DM),
                                       ("down_idx", "d1_dec", DM, 5504)):
            for it in range(nrows // 128):
                def unit_l(it=it, mat=mat, dec=dec, width=width):
                    lut_decode_unit(ctx, block, sb, io[mat], io[dec], it, width)
                fg.q.append(unit_l)

        io1 = dict(io); io1["xT_in"] = io["x0T"]; io1["xT_out"] = io["x1T"]
        build_l0_attn(ctx, block, io1, sb, psum, filler=fa)
        barrier_all(ctx, block)
        io2 = dict(io); io2["xT_in"] = io["x1T"]; io2["xT_out"] = io["x2T"]
        build_l0_ffn(ctx, block, io2, sb, psum, skip_decode=True, filler=fg)
        barrier_all(ctx, block)
        io3 = dict(io); io3["xT_in"] = io["x2T"]; io3["xT_out"] = io["x3T"]
        build_l1_attn(ctx, block, io3, sb, psum)
        barrier_all(ctx, block)
        io4 = dict(io); io4["xT_in"] = io["x3T"]; io4["xT_out"] = io["yT"]
        build_l1_ffn(ctx, block, io4, sb, psum, skip_decode=True)
        finish(ctx, block)
    return nc


def kernel(**inputs):
    x = inputs["L0_attn_hidden_states"]
    xT = np.ascontiguousarray(np.transpose(x, (0, 2, 1)).astype(np.float32, copy=False))
    nc = build_program_fused()
    consts = host_constants()
    in_maps = []
    for cid in range(8):
        m = {"x0T": xT[cid],
             "Wq": inputs["L0_attn_Wq"], "Wk": inputs["L0_attn_Wk"],
             "Wv": inputs["L0_attn_Wv"], "Wo": inputs["L0_attn_Wo"],
             "Wgslc": inputs["L0_attn_Wgslc"], "Wgswa": inputs["L0_attn_Wgswa"],
             "gate_W_packed": inputs["L0_ffn_gate_W_packed"],
             "gate_scales_n": inputs["L0_ffn_gate_scales"],
             "up_W_packed": inputs["L0_ffn_up_W_packed"],
             "up_scales_n": inputs["L0_ffn_up_scales"],
             "down_W_packed": inputs["L0_ffn_down_W_packed"],
             "down_scales_n": inputs["L0_ffn_down_scales"],
             "Wd": inputs["L1_attn_Wd"], "Wq1": inputs["L1_attn_Wq"],
             "Wk1": inputs["L1_attn_Wk"], "Wv1": inputs["L1_attn_Wv"],
             "Wo1": inputs["L1_attn_Wo"],
             "lut": np.ascontiguousarray(inputs["L1_ffn_lut"].reshape(1, 16)),
             "gate_idx": inputs["L1_ffn_gate_idx"], "up_idx": inputs["L1_ffn_up_idx"],
             "down_idx": inputs["L1_ffn_down_idx"]}
        for k, v in consts.items():
            m["c_" + k] = v
        in_maps.append(m)
    res = run_bass_kernel_spmd(nc, in_maps, core_ids=list(range(8)))
    yT = np.stack([res.results[i]["yT"] for i in range(8)])
    return np.ascontiguousarray(np.transpose(yT, (0, 2, 1)))



# version 09 (final)
# speedup vs baseline: 1.1409x; 1.1409x over previous
import math
import numpy as np
import ml_dtypes
import concourse.bass as bass
import concourse.mybir as mybir
from concourse.bass_utils import run_bass_kernel_spmd

F32 = mybir.dt.float32
BF16 = mybir.dt.bfloat16
U16 = mybir.dt.uint16
I32 = mybir.dt.int32
AF = mybir.ActivationFunctionType
ALU = mybir.AluOpType
AX = mybir.AxisListType

T = 2048
DM = 2048
NH, NKV, HDIM = 16, 4, 128
TC = 512
NT = T // TC
NC = DM // 128
INVSQ = 1.0 / math.sqrt(128.0)
LN_EPS = 1e-5
BF = ml_dtypes.bfloat16


def host_constants():
    d = {}
    d["idn_f32"] = np.eye(128, dtype=np.float32)
    d["idn_bf"] = np.eye(128, dtype=BF)
    R = np.zeros((128, 128), dtype=np.float64)
    for i in range(128):
        R[i, (i + 64) % 128] = 1.0
    d["rswap_bf"] = R.astype(BF)
    inv = 1.0 / 10000.0 ** (np.arange(0, 128, 2) / 128.0)
    t = np.arange(T, dtype=np.float64)
    fr = np.outer(inv, t)
    cos, sin = np.cos(fr), np.sin(fr)
    d["cosT"] = np.concatenate([cos, cos], 0).astype(np.float32)
    d["sinTs"] = np.concatenate([-sin, sin], 0).astype(np.float32)
    m = (np.arange(128)[None, :] >= np.arange(128)[:, None])
    d["mask_bf"] = m.astype(BF)
    d["ones_col_bf"] = np.ones((128, 1), dtype=BF)
    d["ones_row_bf"] = np.ones((1, 128), dtype=BF)
    d["ones_col_f32"] = np.ones((128, 1), dtype=np.float32)
    d["ones_row_f32"] = np.ones((1, 128), dtype=np.float32)
    d["eps_f32"] = np.full((1, 1), LN_EPS, dtype=np.float32)
    return d


CONST_DTYPES = {"idn_f32": F32, "idn_bf": BF16, "rswap_bf": BF16, "cosT": F32,
                "sinTs": F32, "mask_bf": BF16, "ones_col_bf": BF16,
                "ones_row_bf": BF16, "ones_col_f32": F32, "ones_row_f32": F32,
                "eps_f32": F32}
CONST_SHAPES = {"idn_f32": [128, 128], "idn_bf": [128, 128], "rswap_bf": [128, 128],
                "cosT": [128, T], "sinTs": [128, T], "mask_bf": [128, 128],
                "ones_col_bf": [128, 1], "ones_row_bf": [1, 128],
                "ones_col_f32": [128, 1], "ones_row_f32": [1, 128], "eps_f32": [1, 1]}


class Fillers:
    def __init__(self):
        self.q = []

    def run(self, n=1):
        k = 0
        while self.q and k < n:
            self.q.pop(0)()
            k += 1

    def drain(self):
        while self.q:
            self.q.pop(0)()


class Ctx:
    def __init__(self, nc):
        self.nc = nc
        self.sem = {}
        self.cnt = {}
        for name in ("ld", "wdma", "st", "pe", "act", "dve", "pool", "fld"):
            self.sem[name] = nc.alloc_semaphore("s_" + name)
            self.cnt[name] = 0

    def inc(self, name, instr, k=1):
        instr.then_inc(self.sem[name], k)
        self.cnt[name] += k
        return self.cnt[name]

    def wait(self, eng, name, val):
        if val > 0:
            eng.wait_ge(self.sem[name], val)

    def dgate(self, dve, sb):
        return self.inc("dve", dve.tensor_copy(sb["dly"][0:1, 32:64],
                                               sb["dly"][0:1, 0:32]))

    def fence(self, eng, name, instr):
        self.inc(name, instr)
        eng.wait_ge(self.sem[name], self.cnt[name])

    def base(self):
        return dict(self.cnt)


def alloc_common(nc):
    sb = {}
    sb["xbT"] = nc.alloc_sbuf_tensor("xbT", [128, NC * T], BF16)
    sb["prhs"] = nc.alloc_sbuf_tensor("prhs", [128, NC * TC], BF16)
    un1 = nc.alloc_sbuf_tensor("un1", [128, 2 * NC * TC], BF16)
    sb["un1"] = un1
    ap1 = un1[:]
    sb["kT"] = ap1[:, 0:NKV * T]
    sb["V"] = ap1[:, NKV * T:2 * NKV * T]
    sb["resid"] = ap1.bitcast(F32)
    sb["wslab0"] = nc.alloc_sbuf_tensor("wslab0", [128, NC * TC], BF16)
    sb["wslab1"] = nc.alloc_sbuf_tensor("wslab1", [128, NC * TC], BF16)
    sb["qh0"] = nc.alloc_sbuf_tensor("qh0", [128, T], BF16)
    sb["pt0"] = nc.alloc_sbuf_tensor("pt0", [128, TC], BF16)
    sb["pt1"] = nc.alloc_sbuf_tensor("pt1", [128, TC], BF16)
    sb["fbc"] = nc.alloc_sbuf_tensor("fbc", [128, TC], BF16)
    sb["gbc"] = nc.alloc_sbuf_tensor("gbc", [128, TC], BF16)
    sb["ao0"] = nc.alloc_sbuf_tensor("ao0", [128, TC], BF16)
    sb["ao1"] = nc.alloc_sbuf_tensor("ao1", [128, TC], BF16)
    sb["kbst0"] = nc.alloc_sbuf_tensor("kbst0", [128, TC], BF16)
    sb["kbst1"] = nc.alloc_sbuf_tensor("kbst1", [128, TC], BF16)
    sb["sc1"] = nc.alloc_sbuf_tensor("sc1", [128, TC], F32)
    sb["sc2"] = nc.alloc_sbuf_tensor("sc2", [128, TC], F32)
    sb["gs"] = nc.alloc_sbuf_tensor("gs", [16, T], BF16)
    sb["rows"] = nc.alloc_sbuf_tensor("rows", [1, 3 * TC], F32)
    sb["rstd_bc"] = nc.alloc_sbuf_tensor("rstd_bc", [128, TC], F32)
    sb["mrow_bc"] = nc.alloc_sbuf_tensor("mrow_bc", [128, TC], F32)
    sb["xt_ring"] = nc.alloc_sbuf_tensor("xt_ring", [128, 4 * TC], F32)
    sb["y_ring"] = nc.alloc_sbuf_tensor("y_ring", [128, 2 * TC], F32)
    sb["wg16"] = nc.alloc_sbuf_tensor("wg16", [128, 2 * NC * 16], BF16)
    sb["lutbc"] = nc.alloc_sbuf_tensor("lutbc", [128, 16], F32)
    sb["scl"] = nc.alloc_sbuf_tensor("scl", [128, 512], F32)
    sb["dly"] = nc.alloc_sbuf_tensor("dly", [1, 64], F32)
    for name in CONST_DTYPES:
        sb[name] = nc.alloc_sbuf_tensor("sb_" + name, CONST_SHAPES[name],
                                        CONST_DTYPES[name])
    psum = [nc.alloc_psum_tensor(f"p{i}", [128, TC], F32) for i in range(8)]
    return sb, psum


def declare_consts(nc):
    return {name: nc.dram_tensor("c_" + name, CONST_SHAPES[name], dt,
                                 kind="ExternalInput").ap()
            for name, dt in CONST_DTYPES.items()}


def load_consts(ctx, block, cap, sb):
    @block.sync
    def _(sync):
        for name in CONST_DTYPES:
            ctx.inc("ld", sync.dma_start(out=sb[name][:], in_=cap[name][:, :]), 16)


def proj_resid_ln(ctx, block, sb, psum, src_dram, w_bf_dram, xT_src, store_y,
                  y_bf_dst=None):
    proj_grp_pe = []
    proj_grp_dve = []
    for j in range(NT):
        b = ctx.base()

        prhs_pe_tgt = proj_grp_pe[j * 4 - 1] if j >= 1 else None

        @block.sync
        def _(sync, j=j, b=b, prhs_pe_tgt=prhs_pe_tgt):
            ctx.wait(sync, "pe", b["pe"] if prhs_pe_tgt is None else prhs_pe_tgt)
            ctx.wait(sync, "st", b["st"])
            for c in range(NC):
                ctx.inc("ld", sync.dma_start(
                    out=sb["prhs"][:, c * TC:(c + 1) * TC],
                    in_=src_dram[c * 128:(c + 1) * 128, j * TC:(j + 1) * TC]), 16)

        for hg in range(4):
            slab = sb["wslab0"] if (j * 4 + hg) % 2 == 0 else sb["wslab1"]
            b = ctx.base()
            g = j * 4 + hg
            pe_tgt = proj_grp_pe[g - 2] if g >= 2 else b["pe"]
            dve_tgt = proj_grp_dve[g - 1] if g >= 1 else b["dve"]

            @block.sync
            def _(sync, j=j, hg=hg, slab=slab, b=b, pe_tgt=pe_tgt, dve_tgt=dve_tgt):
                ctx.wait(sync, "pe", pe_tgt)
                ctx.wait(sync, "dve", dve_tgt)
                for c in range(NC):
                    ctx.inc("ld", sync.dma_start(
                        out=slab[:, c * TC:(c + 1) * TC],
                        in_=w_bf_dram[hg * 512:(hg + 1) * 512, c * 128:(c + 1) * 128],
                        transpose=True), 16)
                for u in range(4):
                    q = hg * 4 + u
                    ctx.inc("ld", sync.dma_start(
                        out=sb["xt_ring"][:, u * TC:(u + 1) * TC],
                        in_=xT_src[q * 128:(q + 1) * 128, j * TC:(j + 1) * TC]), 16)

            b2 = ctx.base()

            @block.tensor
            def _(pe, j=j, hg=hg, slab=slab, b2=b2):
                ctx.wait(pe, "ld", b2["ld"])
                ctx.wait(pe, "dve", b2["dve"])
                for u in range(4):
                    if u >= 2:
                        ctx.wait(pe, "dve", b2["dve"] + u - 1)
                    for c in range(NC):
                        ins = pe.matmul(psum[u % 2][:],
                                        slab[:, c * TC + u * 128:c * TC + (u + 1) * 128],
                                        sb["prhs"][:, c * TC:(c + 1) * TC],
                                        start=(c == 0), stop=(c == NC - 1))
                    ctx.inc("pe", ins)

            b3 = ctx.base()

            @block.vector
            def _(dve, j=j, hg=hg, b3=b3):
                for u in range(4):
                    q = hg * 4 + u
                    ctx.wait(dve, "pe", b3["pe"] - 4 + u + 1)
                    ctx.inc("dve", dve.tensor_tensor(
                        sb["resid"][:, q * TC:(q + 1) * TC], psum[u % 2][:],
                        sb["xt_ring"][:, u * TC:(u + 1) * TC], ALU.add))

            b4 = ctx.base()
            pe_stats_base = ctx.cnt["pe"]

            @block.scalar
            def _(act, j=j, hg=hg, b4=b4, pe_stats_base=pe_stats_base):
                for u in range(4):
                    q = hg * 4 + u
                    ctx.wait(act, "dve", b4["dve"] - 4 + u + 1)
                    if u >= 2:
                        ctx.wait(act, "pe", pe_stats_base + (u - 2) + 1)
                    ctx.inc("act", act.activation(
                        (sb["sc1"] if u % 2 == 0 else sb["sc2"])[:],
                        sb["resid"][:, q * TC:(q + 1) * TC], AF.Square))

            b5 = ctx.base()

            @block.tensor
            def _(pe, j=j, hg=hg, b5=b5):
                for u in range(4):
                    q = hg * 4 + u
                    ctx.wait(pe, "dve", b5["dve"] - 4 + u + 1)
                    i1 = pe.matmul(psum[2][0:1, :], sb["ones_col_f32"][:],
                                   sb["resid"][:, q * TC:(q + 1) * TC],
                                   start=(q == 0), stop=(q == NC - 1))
                    ctx.wait(pe, "act", b5["act"] - 4 + u + 1)
                    i2 = pe.matmul(psum[3][0:1, :], sb["ones_col_f32"][:],
                                   (sb["sc1"] if u % 2 == 0 else sb["sc2"])[:],
                                   start=(q == 0), stop=(q == NC - 1))
                    ctx.inc("pe", i2)

            proj_grp_pe.append(ctx.cnt["pe"])
            proj_grp_dve.append(ctx.cnt["dve"])

        rows = sb["rows"]
        b = ctx.base()

        @block.vector
        def _(dve, j=j, b=b):
            ctx.wait(dve, "pe", b["pe"])
            ctx.fence(dve, "dve", dve.tensor_scalar(rows[0:1, 0:TC], psum[2][0:1, :],
                                                    1.0 / DM, None, ALU.mult))
            ctx.fence(dve, "dve", dve.tensor_scalar(rows[0:1, TC:2 * TC],
                                                    psum[3][0:1, :], 1.0 / DM, None,
                                                    ALU.mult))
            ctx.fence(dve, "dve", dve.tensor_tensor(rows[0:1, 2 * TC:3 * TC],
                                                    rows[0:1, 0:TC], rows[0:1, 0:TC],
                                                    ALU.mult))
            ctx.inc("dve", dve.tensor_tensor(rows[0:1, TC:2 * TC],
                                             rows[0:1, TC:2 * TC],
                                             rows[0:1, 2 * TC:3 * TC], ALU.subtract))

        b = ctx.base()

        @block.scalar
        def _(act, j=j, b=b):
            ctx.wait(act, "dve", b["dve"])
            ctx.inc("act", act.activation(rows[0:1, 2 * TC:3 * TC],
                                          rows[0:1, TC:2 * TC], AF.Sqrt,
                                          bias=sb["eps_f32"][0:1, 0:1]))

        b = ctx.base()

        @block.vector
        def _(dve, j=j, b=b):
            ctx.wait(dve, "act", b["act"])
            ctx.fence(dve, "dve", dve.reciprocal(rows[0:1, TC:2 * TC],
                                                 rows[0:1, 2 * TC:3 * TC]))
            ctx.inc("dve", dve.tensor_tensor(rows[0:1, 2 * TC:3 * TC], rows[0:1, 0:TC],
                                             rows[0:1, TC:2 * TC], ALU.mult))

        b = ctx.base()

        @block.tensor
        def _(pe, j=j, b=b):
            ctx.wait(pe, "dve", b["dve"])
            ctx.inc("pe", pe.matmul(psum[4][:], sb["ones_row_f32"][:],
                                    rows[0:1, TC:2 * TC], start=True, stop=True))
            ctx.inc("pe", pe.matmul(psum[5][:], sb["ones_row_f32"][:],
                                    rows[0:1, 2 * TC:3 * TC], start=True, stop=True))

        b = ctx.base()

        @block.scalar
        def _(act, j=j, b=b):
            ctx.wait(act, "pe", b["pe"])
            act.copy(sb["rstd_bc"][:], psum[4][:])
            ctx.inc("act", act.copy(sb["mrow_bc"][:], psum[5][:]))

        b = ctx.base()
        npq = 2 if y_bf_dst is not None else 1

        @block.vector
        def _(dve, j=j, b=b, npq=npq):
            ctx.wait(dve, "act", b["act"])
            for q in range(NC):
                slot = q % 2
                if q >= 2:
                    ctx.wait(dve, "st", b["st"] + (q - 1) * 16)
                dve.tensor_tensor(sb["sc1"][:], sb["resid"][:, q * TC:(q + 1) * TC],
                                  sb["rstd_bc"][:], ALU.mult)
                dve.tensor_tensor(
                    sb["y_ring"][:, slot * TC:(slot + 1) * TC], sb["sc1"][:],
                    sb["mrow_bc"][:], ALU.subtract)
                if y_bf_dst is not None:
                    dve.tensor_copy(
                        y_bf_dst[:, q * T + j * TC:q * T + (j + 1) * TC],
                        sb["y_ring"][:, slot * TC:(slot + 1) * TC])
                ctx.dgate(dve, sb)

        b = ctx.base()

        @block.sync
        def _(sync, j=j, b=b, npq=npq):
            for q in range(NC):
                slot = q % 2
                ctx.wait(sync, "dve", b["dve"] - NC + q + 1)
                ctx.inc("st", store_y(sync, j, q,
                                      sb["y_ring"][:, slot * TC:(slot + 1) * TC]), 16)




def _dump(ctx, block, dbg_ap, src_ap, pad16=False):
    b = ctx.base()

    @block.sync
    def _(sync, b=b):
        ctx.wait(sync, "dve", b["dve"])
        ctx.wait(sync, "act", b["act"])
        ctx.wait(sync, "pe", b["pe"])
        if pad16:
            ctx.inc("st", sync.dma_start(out=dbg_ap[0:16, 0:src_ap.shape[1]],
                                         in_=src_ap), 16)
        else:
            ctx.inc("st", sync.dma_start(out=dbg_ap[:, 0:src_ap.shape[1]],
                                         in_=src_ap), 16)

def build_l0_attn(ctx, block, io, sb, psum, stage=35, filler=None):
    xbT, kT, V = sb["xbT"], sb["kT"], sb["V"]

    @block.gpsimd
    def _(g):
        ctx.inc("wdma", g.dma_start(out=io["wk_bf"][:, :], in_=io["Wk"][:, :]), 16)
        ctx.inc("wdma", g.dma_start(out=io["wv_bf"][:, :], in_=io["Wv"][:, :]), 16)
        ctx.inc("wdma", g.dma_start(out=io["wq_bf"][:, :], in_=io["Wq"][:, :]), 16)
        ctx.inc("wdma", g.dma_start(out=io["wo_bf"][:, :], in_=io["Wo"][:, :]), 16)
        ctx.inc("wdma", g.dma_start(out=io["wgs_bf"][0:16, :], in_=io["Wgslc"][:, :]), 16)
        ctx.inc("wdma", g.dma_start(out=io["wgs_bf"][16:32, :], in_=io["Wgswa"][:, :]), 16)
        for c in range(NC):
            ctx.inc("wdma", g.dma_start(out=xbT[:, c * T:(c + 1) * T],
                                        in_=io["xT_in"][c * 128:(c + 1) * 128, :]), 16)

    b = ctx.base()

    @block.sync
    def _(sync, b=b):
        ctx.wait(sync, "wdma", b["wdma"])
        for w in range(2):
            for c in range(NC):
                ctx.inc("ld", sync.dma_start(
                    out=sb["wg16"][:, (w * NC + c) * 16:(w * NC + c + 1) * 16],
                    in_=io["wgs_bf"][w * 16:(w + 1) * 16, c * 128:(c + 1) * 128],
                    transpose=True), 16)
        for c in range(NC):
            ctx.inc("ld", sync.dma_start(out=sb["wslab0"][:, c * TC:(c + 1) * TC],
                                         in_=io["wk_bf"][0:512, c * 128:(c + 1) * 128],
                                         transpose=True), 16)

    items_k = [(kv, tc) for kv in range(NKV) for tc in range(NT)]
    b = ctx.base()

    @block.tensor
    def _(pe, b=b):
        ctx.wait(pe, "ld", b["ld"])
        ctx.wait(pe, "wdma", b["wdma"])
        for m, (kv, tc) in enumerate(items_k):
            ctx.wait(pe, "act", b["act"] + max(m - 1, 0))
            for c in range(NC):
                ins = pe.matmul(psum[m % 2][:],
                                sb["wslab0"][:, c * TC + kv * 128:c * TC + (kv + 1) * 128],
                                xbT[:, c * T + tc * TC:c * T + (tc + 1) * TC],
                                start=(c == 0), stop=(c == NC - 1))
            ctx.inc("pe", ins)
            ctx.wait(pe, "act", b["act"] + m + 1)
            ctx.wait(pe, "dve", b["dve"] + m - 1)
            ctx.inc("pe", pe.matmul(psum[4 + m % 2][:], sb["rswap_bf"][:],
                                    (sb["kbst0"] if m % 2 == 0 else sb["kbst1"])[:],
                                    start=True, stop=True))

    b2 = ctx.base()

    @block.scalar
    def _(act, b2=b2):
        for m in range(len(items_k)):
            ctx.wait(act, "pe", b2["pe"] - 32 + 2 * m + 1)
            if m >= 2:
                ctx.wait(act, "dve", b2["dve"] + m - 1)
            ctx.inc("act", act.copy((sb["kbst0"] if m % 2 == 0 else sb["kbst1"])[:],
                                    psum[m % 2][:]))

    b3 = ctx.base()

    @block.vector
    def _(dve, b3=b3):
        for m, (kv, tc) in enumerate(items_k):
            ctx.wait(dve, "pe", b3["pe"] - 32 + 2 * m + 2)
            stage = sb["kbst0"] if m % 2 == 0 else sb["kbst1"]
            dve.tensor_tensor(sb["sc1"][:], stage[:],
                              sb["cosT"][:, tc * TC:(tc + 1) * TC], ALU.mult)
            dve.tensor_tensor(sb["sc2"][:], psum[4 + m % 2][:],
                              sb["sinTs"][:, tc * TC:(tc + 1) * TC], ALU.mult)
            ctx.inc("dve", dve.tensor_tensor(kT[:, kv * T + tc * TC:kv * T + (tc + 1) * TC],
                                             sb["sc1"][:], sb["sc2"][:], ALU.add))

    if stage <= 1:
        _dump(ctx, block, io["dbg"], sb["kT"])
        return

    b = ctx.base()

    @block.sync
    def _(sync, b=b):
        for c in range(NC):
            ctx.inc("ld", sync.dma_start(out=sb["wslab1"][:, c * TC:(c + 1) * TC],
                                         in_=io["wv_bf"][0:512, c * 128:(c + 1) * 128],
                                         transpose=True), 16)

    b = ctx.base()

    @block.tensor
    def _(pe, b=b):
        ctx.wait(pe, "ld", b["ld"])
        for st in range(NC):
            ctx.wait(pe, "act", b["act"] + st - 1)
            for c in range(NC):
                ins = pe.matmul(psum[6 + st % 2][:],
                                xbT[:, c * T + st * 128:c * T + (st + 1) * 128],
                                sb["wslab1"][:, c * TC:(c + 1) * TC],
                                start=(c == 0), stop=(c == NC - 1))
            ctx.inc("pe", ins)

    b = ctx.base()

    @block.scalar
    def _(act, b=b):
        for st in range(NC):
            ctx.wait(act, "pe", b["pe"] - NC + st + 1)
            ctx.inc("act", act.copy(V[:, st * TC:(st + 1) * TC], psum[6 + st % 2][:]))

    if stage <= 2:
        _dump(ctx, block, io["dbg"], sb["V"])
        return

    ngate_tc = {21: 1, 22: 4}.get(stage, NT)
    skip_gadd = (stage in (21, 22))

    b = ctx.base()

    @block.tensor
    def _(pe, b=b):
        for tc in range(ngate_tc):
            ctx.wait(pe, "act", b["act"] + tc)
            for w in range(2):
                for c in range(NC):
                    ins = pe.matmul(psum[2 + w][0:16, :],
                                    sb["wg16"][:, (w * NC + c) * 16:(w * NC + c + 1) * 16],
                                    xbT[:, c * T + tc * TC:c * T + (tc + 1) * TC],
                                    start=(c == 0), stop=(c == NC - 1))
                ctx.inc("pe", ins)

    b = ctx.base()

    @block.scalar
    def _(act, b=b):
        for tc in range(ngate_tc):
            ctx.wait(act, "pe", b["pe"] - 2 * ngate_tc + 2 * tc + 2)
            if not skip_gadd:
                ctx.wait(act, "dve", b["dve"] + tc)
            act.activation(sb["gs"][0:16, tc * TC:(tc + 1) * TC],
                           psum[2][0:16, :], AF.Sigmoid)
            ctx.inc("act", act.activation(sb["fbc"][0:16, :],
                                          psum[3][0:16, :], AF.Sigmoid))

    b = ctx.base()

    @block.vector
    def _(dve, b=b):
        if skip_gadd:
            return
        for tc in range(ngate_tc):
            ctx.wait(dve, "act", b["act"] - ngate_tc + tc + 1)
            ctx.inc("dve", dve.tensor_tensor(sb["gs"][0:16, tc * TC:(tc + 1) * TC],
                                             sb["gs"][0:16, tc * TC:(tc + 1) * TC],
                                             sb["fbc"][0:16, :], ALU.add))

    if stage <= 23:
        _dump(ctx, block, io["dbg"], sb["gs"][0:16, :], pad16=True)
        return

    attn_dve_mark = [0]
    nst = [0]
    ao_marks = []
    for grp in range(NKV):
        b = ctx.base()

        @block.sync
        def _(sync, grp=grp, b=b):
            ctx.wait(sync, "pe", b["pe"])
            for c in range(NC):
                ctx.inc("ld", sync.dma_start(
                    out=sb["wslab0"][:, c * TC:(c + 1) * TC],
                    in_=io["wq_bf"][grp * 512:(grp + 1) * 512, c * 128:(c + 1) * 128],
                    transpose=True), 16)

        for u in range(4):
            h = grp * 4 + u
            kv = grp
            qh = sb["qh0"]

            b = ctx.base()

            qm = attn_dve_mark[0]

            @block.tensor
            def _(pe, u=u, qh=qh, b=b, qm=qm):
                ctx.wait(pe, "ld", b["ld"])
                ctx.wait(pe, "dve", qm if qm else b["dve"])
                for tc in range(NT):
                    ctx.wait(pe, "act", b["act"] + tc - 1)
                    for c in range(NC):
                        ins = pe.matmul(psum[tc % 2][:],
                                        sb["wslab0"][:, c * TC + u * 128:c * TC + (u + 1) * 128],
                                        xbT[:, c * T + tc * TC:c * T + (tc + 1) * TC],
                                        start=(c == 0), stop=(c == NC - 1))
                    ctx.inc("pe", ins)
                    ctx.wait(pe, "act", b["act"] + tc + 1)
                    if tc < 2 and qm:
                        ctx.wait(pe, "dve", qm)
                    else:
                        ctx.wait(pe, "dve", b["dve"] + tc - 1)
                    ctx.inc("pe", pe.matmul(psum[4 + tc % 2][:], sb["rswap_bf"][:],
                                            (sb["kbst0"] if tc % 2 == 0 else sb["kbst1"])[:],
                                            start=True, stop=True))

            b2 = ctx.base()

            @block.scalar
            def _(act, b2=b2):
                for tc in range(NT):
                    ctx.wait(act, "pe", b2["pe"] - 8 + 2 * tc + 1)
                    if tc >= 2:
                        ctx.wait(act, "dve", b2["dve"] + tc - 1)
                    ctx.inc("act", act.copy((sb["kbst0"] if tc % 2 == 0 else sb["kbst1"])[:],
                                            psum[tc % 2][:]))

            b3 = ctx.base()

            @block.vector
            def _(dve, qh=qh, b3=b3):
                for tc in range(NT):
                    ctx.wait(dve, "pe", b3["pe"] - 8 + 2 * tc + 2)
                    stage = sb["kbst0"] if tc % 2 == 0 else sb["kbst1"]
                    dve.tensor_tensor(sb["sc1"][:], stage[:],
                                      sb["cosT"][:, tc * TC:(tc + 1) * TC], ALU.mult)
                    dve.tensor_tensor(sb["sc2"][:], psum[4 + tc % 2][:],
                                      sb["sinTs"][:, tc * TC:(tc + 1) * TC], ALU.mult)
                    ctx.inc("dve", dve.tensor_tensor(qh[:, tc * TC:(tc + 1) * TC],
                                                     sb["sc1"][:], sb["sc2"][:], ALU.add))
            attn_dve_mark[0] = ctx.cnt["dve"]

            for j in range(NT):
                smax = min(NC, 4 * (j + 1))
                minfo = []
                for i in range(smax):
                    zl = i * 128 - j * TC
                    minfo.append((max(zl, 0), 0 <= zl < TC))
                b = ctx.base()

                cm = attn_dve_mark[0]

                @block.tensor
                def _(pe, kv=kv, j=j, qh=qh, smax=smax, minfo=minfo, b=b, cm=cm):
                    ctx.wait(pe, "dve", cm if cm else b["dve"])
                    dcnt = 0
                    for i in range(smax):
                        zl, hd = minfo[i]
                        ctx.wait(pe, "act", b["act"] + i - 1)
                        ctx.inc("pe", pe.matmul(psum[2 + i % 2][:],
                                                kT[:, kv * T + i * 128:kv * T + (i + 1) * 128],
                                                qh[:, j * TC:(j + 1) * TC],
                                                start=True, stop=True))
                        nm = (1 if zl > 0 else 0) + (1 if hd else 0)
                        if nm:
                            dcnt += nm
                            ctx.wait(pe, "dve", b["dve"] + dcnt)
                        else:
                            ctx.wait(pe, "act", b["act"] + i + 1)
                        pt = sb["pt0"] if i % 2 == 0 else sb["pt1"]
                        pe.matmul(psum[5][:], V[:, i * TC + kv * 128:i * TC + (kv + 1) * 128],
                                  pt[:], start=(i == 0), stop=(i == smax - 1))
                        ctx.inc("pe", pe.matmul(psum[6][0:1, :], sb["ones_col_bf"][:],
                                                pt[:], start=(i == 0), stop=(i == smax - 1)))

                b2 = ctx.base()

                @block.scalar
                def _(act, smax=smax, b2=b2):
                    for i in range(smax):
                        ctx.wait(act, "pe", b2["pe"] - 2 * smax + 2 * i + 1)
                        pt = sb["pt0"] if i % 2 == 0 else sb["pt1"]
                        ctx.inc("act", act.activation(pt[:], psum[2 + i % 2][:],
                                                      AF.Exp, scale=INVSQ))

                b3 = ctx.base()

                @block.vector
                def _(dve, smax=smax, minfo=minfo, b3=b3):
                    for i in range(smax):
                        zl, hd = minfo[i]
                        if zl == 0 and not hd:
                            continue
                        ctx.wait(dve, "act", b3["act"] - smax + i + 1)
                        pt = sb["pt0"] if i % 2 == 0 else sb["pt1"]
                        if zl > 0:
                            ctx.inc("dve", dve.memset(pt[:, 0:zl], 0.0))
                        if hd:
                            ctx.inc("dve", dve.tensor_tensor(pt[:, zl:zl + 128],
                                                             pt[:, zl:zl + 128],
                                                             sb["mask_bf"][:], ALU.mult))

                b = ctx.base()

                @block.vector
                def _(dve, b=b):
                    ctx.wait(dve, "pe", b["pe"])
                    ctx.fence(dve, "dve", dve.tensor_scalar(sb["rows"][0:1, 0:TC],
                                                            psum[6][0:1, :], 1.0, None,
                                                            ALU.mult))
                    ctx.fence(dve, "dve", dve.reciprocal(sb["rows"][0:1, TC:2 * TC],
                                                         sb["rows"][0:1, 0:TC]))
                    ctx.inc("dve", dve.tensor_scalar(sb["fbc"][0:1, :],
                                                     sb["rows"][0:1, TC:2 * TC],
                                                     1.0, None, ALU.mult))

                b = ctx.base()

                @block.tensor
                def _(pe, h=h, j=j, b=b):
                    ctx.wait(pe, "dve", b["dve"])
                    ctx.inc("pe", pe.matmul(psum[7][:], sb["ones_row_bf"][:],
                                            sb["fbc"][0:1, :], start=True, stop=True))
                    ctx.inc("pe", pe.matmul(psum[0][:],
                                            sb["idn_bf"][0:16, h:h + 1].broadcast_to([16, 128]),
                                            sb["gs"][0:16, j * TC:(j + 1) * TC],
                                            start=True, stop=True))

                b = ctx.base()

                @block.scalar
                def _(act, b=b):
                    ctx.wait(act, "pe", b["pe"])
                    act.copy(sb["fbc"][:], psum[7][:])
                    ctx.inc("act", act.copy(sb["gbc"][:], psum[0][:]))

                nastore = nst[0]
                ao = sb["ao0"] if nastore % 2 == 0 else sb["ao1"]
                b = ctx.base()

                @block.vector
                def _(dve, ao=ao, nastore=nastore, b=b):
                    ctx.wait(dve, "act", b["act"])
                    if nastore >= 2:
                        ctx.wait(dve, "st", ao_marks[nastore - 2])
                    dve.tensor_tensor(sb["sc2"][:], psum[5][:], sb["fbc"][:], ALU.mult)
                    dve.tensor_tensor(ao[:], sb["sc2"][:], sb["gbc"][:], ALU.mult)
                    ctx.dgate(dve, sb)

                b = ctx.base()

                @block.sync
                def _(sync, h=h, j=j, ao=ao, b=b):
                    ctx.wait(sync, "dve", b["dve"])
                    ctx.inc("st", sync.dma_start(
                        out=io["attn_bf"][h * 128:(h + 1) * 128, j * TC:(j + 1) * TC],
                        in_=ao[:]), 16)
                ao_marks.append(ctx.cnt["st"])
                nst[0] += 1
                attn_dve_mark[0] = ctx.cnt["dve"]
                if filler is not None:
                    filler.run(5)

    if stage <= 34:
        return

    if filler is not None:
        filler.drain()

    def store_y(sync, j, q, ysl):
        return sync.dma_start(out=io["xT_out"][q * 128:(q + 1) * 128, j * TC:(j + 1) * TC],
                              in_=ysl)

    proj_resid_ln(ctx, block, sb, psum, io["attn_bf"], io["wo_bf"], io["xT_in"],
                  store_y, y_bf_dst=io.get("y_bf_dst"))


def finish(ctx, block):
    b = ctx.base()

    @block.sync
    def _(sync, b=b):
        ctx.wait(sync, "st", b["st"])
        ctx.wait(sync, "wdma", b["wdma"])
        ctx.wait(sync, "dve", b["dve"])
        ctx.wait(sync, "act", b["act"])
        ctx.wait(sync, "pe", b["pe"])


def build_program_l0_attn(stage=35):
    nc = bass.Bass("TRN2", target_bir_lowering=False, debug=False)
    io = {}
    io["xT_in"] = nc.dram_tensor("xT_in", [DM, T], F32, kind="ExternalInput").ap()
    io["Wq"] = nc.dram_tensor("Wq", [DM, DM], F32, kind="ExternalInput").ap()
    io["Wk"] = nc.dram_tensor("Wk", [512, DM], F32, kind="ExternalInput").ap()
    io["Wv"] = nc.dram_tensor("Wv", [512, DM], F32, kind="ExternalInput").ap()
    io["Wo"] = nc.dram_tensor("Wo", [DM, DM], F32, kind="ExternalInput").ap()
    io["Wgslc"] = nc.dram_tensor("Wgslc", [16, DM], F32, kind="ExternalInput").ap()
    io["Wgswa"] = nc.dram_tensor("Wgswa", [16, DM], F32, kind="ExternalInput").ap()
    io["xT_out"] = nc.dram_tensor("xT_out", [DM, T], F32, kind="ExternalOutput").ap()
    io["wq_bf"] = nc.dram_tensor("wq_bf", [DM, DM], BF16, kind="Internal").ap()
    io["wk_bf"] = nc.dram_tensor("wk_bf", [512, DM], BF16, kind="Internal").ap()
    io["wv_bf"] = nc.dram_tensor("wv_bf", [512, DM], BF16, kind="Internal").ap()
    io["wo_bf"] = nc.dram_tensor("wo_bf", [DM, DM], BF16, kind="Internal").ap()
    io["wgs_bf"] = nc.dram_tensor("wgs_bf", [32, DM], BF16, kind="Internal").ap()
    io["attn_bf"] = nc.dram_tensor("attn_bf", [DM, T], BF16,
                                   kind=("ExternalOutput" if stage == 34 else "Internal")).ap()
    io["dbg"] = nc.dram_tensor("dbg", [128, 2 * NC * TC], BF16, kind="ExternalOutput").ap()
    cap = declare_consts(nc)
    sb, psum = alloc_common(nc)
    ctx = Ctx(nc)
    with nc.Block() as block:
        load_consts(ctx, block, cap, sb)
        build_l0_attn(ctx, block, io, sb, psum, stage=stage)
        finish(ctx, block)
    return nc


def _run_l0_attn(xT_all, weights, stage=35):
    nc = build_program_l0_attn(stage=stage)
    consts = host_constants()
    in_maps = []
    for cid in range(8):
        m = {"xT_in": np.ascontiguousarray(xT_all[cid]),
             "Wq": weights["L0_attn_Wq"], "Wk": weights["L0_attn_Wk"],
             "Wv": weights["L0_attn_Wv"], "Wo": weights["L0_attn_Wo"],
             "Wgslc": weights["L0_attn_Wgslc"], "Wgswa": weights["L0_attn_Wgswa"]}
        for k, v in consts.items():
            m["c_" + k] = v
        in_maps.append(m)
    res = run_bass_kernel_spmd(nc, in_maps, core_ids=list(range(8)))
    if stage >= 35:
        return np.stack([res.results[i]["xT_out"] for i in range(8)])
    key = "attn_bf" if stage == 34 else "dbg"
    return np.stack([res.results[i][key] for i in range(8)])



def build_l1_attn(ctx, block, io, sb, psum):
    xbT = sb["xbT"]
    cT = sb["kT"]
    wvT = sb["V"]
    vh = sb["qh0"]
    QaT = sb["prhs"]
    ah = sb["wslab1"]
    wq_kh = sb["xt_ring"][:].bitcast(BF16)

    @block.gpsimd
    def _(g):
        ctx.inc("wdma", g.dma_start(out=io["wd_bf"][:, :], in_=io["Wd"][:, :]), 16)
        ctx.inc("wdma", g.dma_start(out=io["wv1_bf"][:, :], in_=io["Wv1"][:, :]), 16)
        ctx.inc("wdma", g.dma_start(out=io["wq1_bf"][:, :], in_=io["Wq1"][:, :]), 16)
        ctx.inc("wdma", g.dma_start(out=io["wk1_bf"][:, :], in_=io["Wk1"][:, :]), 16)
        ctx.inc("wdma", g.dma_start(out=io["wo1_bf"][:, :], in_=io["Wo1"][:, :]), 16)
        for c in range(NC):
            ctx.inc("wdma", g.dma_start(out=xbT[:, c * T:(c + 1) * T],
                                        in_=io["xT_in"][c * 128:(c + 1) * 128, :]), 16)

    b = ctx.base()

    @block.sync
    def _(sync, b=b):
        ctx.wait(sync, "wdma", b["wdma"])
        for c in range(NC):
            ctx.inc("ld", sync.dma_start(out=sb["wslab0"][:, c * TC:(c + 1) * TC],
                                         in_=io["wd_bf"][0:512, c * 128:(c + 1) * 128],
                                         transpose=True), 16)
        for kt in range(4):
            ctx.inc("ld", sync.dma_start(out=wvT[:, kt * T:(kt + 1) * T],
                                         in_=io["wv1_bf"][0:T, kt * 128:(kt + 1) * 128],
                                         transpose=True), 16)

    items_c = [(kt, tc) for kt in range(4) for tc in range(NT)]
    b = ctx.base()

    @block.tensor
    def _(pe, b=b):
        ctx.wait(pe, "ld", b["ld"])
        ctx.wait(pe, "wdma", b["wdma"])
        for m, (kt, tc) in enumerate(items_c):
            ctx.wait(pe, "act", b["act"] + m - 1)
            for c in range(NC):
                ins = pe.matmul(psum[m % 2][:],
                                sb["wslab0"][:, c * TC + kt * 128:c * TC + (kt + 1) * 128],
                                xbT[:, c * T + tc * TC:c * T + (tc + 1) * TC],
                                start=(c == 0), stop=(c == NC - 1))
            ctx.inc("pe", ins)

    b = ctx.base()

    @block.scalar
    def _(act, b=b):
        for m, (kt, tc) in enumerate(items_c):
            ctx.wait(act, "pe", b["pe"] - 16 + m + 1)
            ctx.inc("act", act.copy(cT[:, kt * T + tc * TC:kt * T + (tc + 1) * TC],
                                    psum[m % 2][:]))

    for h in range(NH):
        b = ctx.base()

        @block.sync
        def _(sync, h=h, b=b):
            ctx.wait(sync, "pe", b["pe"])
            ctx.inc("ld", sync.dma_start(out=wq_kh[:, 0:T],
                                         in_=io["wq1_bf"][h * 128:(h + 1) * 128, :]), 16)
            ctx.inc("ld", sync.dma_start(out=wq_kh[:, T:T + 512],
                                         in_=io["wk1_bf"][h * 128:(h + 1) * 128, :]), 16)

        b = ctx.base()

        @block.tensor
        def _(pe, h=h, b=b):
            ctx.wait(pe, "ld", b["ld"])
            for dc in range(NC):
                ctx.wait(pe, "act", b["act"] + dc - 1)
                ctx.inc("pe", pe.matmul(psum[dc % 2][:],
                                        wq_kh[:, dc * 128:(dc + 1) * 128],
                                        wq_kh[:, T:T + 512],
                                        start=True, stop=True))

        b = ctx.base()

        @block.scalar
        def _(act, b=b):
            for dc in range(NC):
                ctx.wait(act, "pe", b["pe"] - NC + dc + 1)
                ctx.inc("act", act.copy(ah[:, dc * TC:(dc + 1) * TC], psum[dc % 2][:]))

        items_q = [(kt, tc) for kt in range(4) for tc in range(NT)]
        b = ctx.base()

        @block.tensor
        def _(pe, b=b):
            ctx.wait(pe, "act", b["act"])
            ctx.wait(pe, "dve", b["dve"])
            for m, (kt, tc) in enumerate(items_q):
                ctx.wait(pe, "act", b["act"] + m - 1)
                for dc in range(NC):
                    ins = pe.matmul(psum[m % 2][:],
                                    ah[:, dc * TC + kt * 128:dc * TC + (kt + 1) * 128],
                                    xbT[:, dc * T + tc * TC:dc * T + (tc + 1) * TC],
                                    start=(dc == 0), stop=(dc == NC - 1))
                ctx.inc("pe", ins)

        b = ctx.base()

        @block.scalar
        def _(act, b=b):
            for m, (kt, tc) in enumerate(items_q):
                ctx.wait(act, "pe", b["pe"] - 16 + m + 1)
                ctx.inc("act", act.copy(QaT[:, kt * T + tc * TC:kt * T + (tc + 1) * TC],
                                        psum[m % 2][:]))

        b = ctx.base()

        @block.tensor
        def _(pe, h=h, b=b):
            ctx.wait(pe, "dve", b["dve"])
            for st in range(NC):
                ctx.wait(pe, "act", b["act"] + st - 1)
                for kt in range(4):
                    ins = pe.matmul(psum[2 + st % 2][0:128, 0:128],
                                    cT[:, kt * T + st * 128:kt * T + (st + 1) * 128],
                                    wvT[:, kt * T + h * 128:kt * T + (h + 1) * 128],
                                    start=(kt == 0), stop=(kt == 3))
                ctx.inc("pe", ins)

        b = ctx.base()

        @block.scalar
        def _(act, b=b):
            for st in range(NC):
                ctx.wait(act, "pe", b["pe"] - NC + st + 1)
                ctx.inc("act", act.copy(vh[:, st * 128:(st + 1) * 128],
                                        psum[2 + st % 2][0:128, 0:128]))

        for j in range(NT):
            smax = min(NC, 4 * (j + 1))
            minfo = []
            for i in range(smax):
                zl = i * 128 - j * TC
                minfo.append((max(zl, 0), 0 <= zl < TC))
            b = ctx.base()

            @block.tensor
            def _(pe, j=j, smax=smax, minfo=minfo, b=b):
                ctx.wait(pe, "dve", b["dve"])
                ctx.wait(pe, "act", b["act"])
                dcnt = 0
                for i in range(smax):
                    zl, hd = minfo[i]
                    ctx.wait(pe, "act", b["act"] + i - 1)
                    for kt in range(4):
                        ins = pe.matmul(psum[4 + i % 2][:],
                                        cT[:, kt * T + i * 128:kt * T + (i + 1) * 128],
                                        QaT[:, kt * T + j * TC:kt * T + (j + 1) * TC],
                                        start=(kt == 0), stop=(kt == 3))
                    ctx.inc("pe", ins)
                    nm = (1 if zl > 0 else 0) + (1 if hd else 0)
                    if nm:
                        dcnt += nm
                        ctx.wait(pe, "dve", b["dve"] + dcnt)
                    else:
                        ctx.wait(pe, "act", b["act"] + i + 1)
                    pt = sb["pt0"] if i % 2 == 0 else sb["pt1"]
                    pe.matmul(psum[6][:], vh[:, i * 128:(i + 1) * 128],
                              pt[:], start=(i == 0), stop=(i == smax - 1))
                    ctx.inc("pe", pe.matmul(psum[7][0:1, :], sb["ones_col_bf"][:],
                                            pt[:], start=(i == 0), stop=(i == smax - 1)))

            b2 = ctx.base()

            @block.scalar
            def _(act, smax=smax, b2=b2):
                for i in range(smax):
                    ctx.wait(act, "pe", b2["pe"] - 2 * smax + 2 * i + 1)
                    pt = sb["pt0"] if i % 2 == 0 else sb["pt1"]
                    ctx.inc("act", act.activation(pt[:], psum[4 + i % 2][:],
                                                  AF.Exp, scale=INVSQ))

            b3 = ctx.base()

            @block.vector
            def _(dve, smax=smax, minfo=minfo, b3=b3):
                for i in range(smax):
                    zl, hd = minfo[i]
                    if zl == 0 and not hd:
                        continue
                    ctx.wait(dve, "act", b3["act"] - smax + i + 1)
                    pt = sb["pt0"] if i % 2 == 0 else sb["pt1"]
                    if zl > 0:
                        ctx.inc("dve", dve.memset(pt[:, 0:zl], 0.0))
                    if hd:
                        ctx.inc("dve", dve.tensor_tensor(pt[:, zl:zl + 128],
                                                         pt[:, zl:zl + 128],
                                                         sb["mask_bf"][:], ALU.mult))

            b = ctx.base()

            @block.vector
            def _(dve, b=b):
                ctx.wait(dve, "pe", b["pe"])
                ctx.fence(dve, "dve", dve.tensor_scalar(sb["rows"][0:1, 0:TC],
                                                        psum[7][0:1, :], 1.0, None,
                                                        ALU.mult))
                ctx.fence(dve, "dve", dve.reciprocal(sb["rows"][0:1, TC:2 * TC],
                                                     sb["rows"][0:1, 0:TC]))
                ctx.inc("dve", dve.tensor_scalar(sb["fbc"][0:1, :],
                                                 sb["rows"][0:1, TC:2 * TC],
                                                 1.0, None, ALU.mult))

            b = ctx.base()

            @block.tensor
            def _(pe, b=b):
                ctx.wait(pe, "dve", b["dve"])
                ctx.inc("pe", pe.matmul(psum[3][:], sb["ones_row_bf"][:],
                                        sb["fbc"][0:1, :], start=True, stop=True))

            b = ctx.base()

            @block.scalar
            def _(act, b=b):
                ctx.wait(act, "pe", b["pe"])
                ctx.inc("act", act.copy(sb["fbc"][:], psum[3][:]))

            nastore = ctx.cnt["st"]
            b = ctx.base()

            @block.vector
            def _(dve, b=b):
                ctx.wait(dve, "act", b["act"])
                ctx.wait(dve, "st", b["st"])
                dve.tensor_tensor(sb["ao0"][:], psum[6][:], sb["fbc"][:], ALU.mult)
                ctx.dgate(dve, sb)

            b = ctx.base()

            @block.sync
            def _(sync, h=h, j=j, b=b):
                ctx.wait(sync, "dve", b["dve"])
                ctx.inc("st", sync.dma_start(
                    out=io["attn_bf"][h * 128:(h + 1) * 128, j * TC:(j + 1) * TC],
                    in_=sb["ao0"][:]), 16)

    def store_y(sync, j, q, ysl):
        return sync.dma_start(out=io["xT_out"][q * 128:(q + 1) * 128, j * TC:(j + 1) * TC],
                              in_=ysl)

    proj_resid_ln(ctx, block, sb, psum, io["attn_bf"], io["wo1_bf"], io["xT_in"],
                  store_y, y_bf_dst=io.get("y_bf_dst"))


def build_program_l1_attn():
    nc = bass.Bass("TRN2", target_bir_lowering=False, debug=False)
    io = {}
    io["xT_in"] = nc.dram_tensor("xT_in", [DM, T], F32, kind="ExternalInput").ap()
    io["Wd"] = nc.dram_tensor("Wd", [512, DM], F32, kind="ExternalInput").ap()
    io["Wq1"] = nc.dram_tensor("Wq1", [DM, DM], F32, kind="ExternalInput").ap()
    io["Wk1"] = nc.dram_tensor("Wk1", [DM, 512], F32, kind="ExternalInput").ap()
    io["Wv1"] = nc.dram_tensor("Wv1", [DM, 512], F32, kind="ExternalInput").ap()
    io["Wo1"] = nc.dram_tensor("Wo1", [DM, DM], F32, kind="ExternalInput").ap()
    io["xT_out"] = nc.dram_tensor("xT_out", [DM, T], F32, kind="ExternalOutput").ap()
    io["wd_bf"] = nc.dram_tensor("wd_bf", [512, DM], BF16, kind="Internal").ap()
    io["wq1_bf"] = nc.dram_tensor("wq1_bf", [DM, DM], BF16, kind="Internal").ap()
    io["wk1_bf"] = nc.dram_tensor("wk1_bf", [DM, 512], BF16, kind="Internal").ap()
    io["wv1_bf"] = nc.dram_tensor("wv1_bf", [DM, 512], BF16, kind="Internal").ap()
    io["wo1_bf"] = nc.dram_tensor("wo1_bf", [DM, DM], BF16, kind="Internal").ap()
    io["attn_bf"] = nc.dram_tensor("attn_bf", [DM, T], BF16, kind="Internal").ap()
    cap = declare_consts(nc)
    sb, psum = alloc_common(nc)
    ctx = Ctx(nc)
    with nc.Block() as block:
        load_consts(ctx, block, cap, sb)
        build_l1_attn(ctx, block, io, sb, psum)
        finish(ctx, block)
    return nc


def _run_l1_attn(xT_all, weights):
    nc = build_program_l1_attn()
    consts = host_constants()
    in_maps = []
    for cid in range(8):
        m = {"xT_in": np.ascontiguousarray(xT_all[cid]),
             "Wd": weights["L1_attn_Wd"], "Wq1": weights["L1_attn_Wq"],
             "Wk1": weights["L1_attn_Wk"], "Wv1": weights["L1_attn_Wv"],
             "Wo1": weights["L1_attn_Wo"]}
        for k, v in consts.items():
            m["c_" + k] = v
        in_maps.append(m)
    res = run_bass_kernel_spmd(nc, in_maps, core_ids=list(range(8)))
    return np.stack([res.results[i]["xT_out"] for i in range(8)])



def lut_decode_unit(ctx, block, sb, idx_dram, dec_dram, it, width,
                    bufs=None):
    if bufs is None:
        idx_u16 = sb["prhs"][:].bitcast(U16)
        acc = sb["un1"][:, 0:width]
        tmp = sb["un1"][:, width:2 * width]
    else:
        idx_u16, acc, tmp = bufs
    b = ctx.base()

    @block.gpsimd
    def _(g, it=it, b=b):
        ctx.wait(g, "dve", b["dve"])
        ctx.inc("fld", g.dma_start(out=idx_u16[:, 0:width],
                                   in_=idx_dram[it * 128:(it + 1) * 128, :]), 16)
        ctx.inc("fld", g.dma_start(out=sb["dly"][0:1, 0:32],
                                   in_=idx_dram[0:1, 0:32]), 16)

    b = ctx.base()

    @block.vector
    def _(dve, it=it, b=b):
        ctx.wait(dve, "fld", b["fld"])
        ctx.wait(dve, "act", b["act"])
        ctx.wait(dve, "st", b["st"])
        dve.tensor_scalar(acc, idx_u16[:, 0:width], 0.0, sb["lutbc"][:, 0:1],
                          ALU.is_equal, ALU.mult)
        for j in range(1, 16):
            dve.tensor_scalar(tmp, idx_u16[:, 0:width], float(j),
                              sb["lutbc"][:, j:j + 1], ALU.is_equal, ALU.mult)
            dve.tensor_tensor(acc, acc, tmp, ALU.add)
        ctx.dgate(dve, sb)

    b = ctx.base()

    @block.gpsimd
    def _(g, it=it, b=b):
        ctx.wait(g, "dve", b["dve"])
        ctx.inc("st", g.dma_start(out=dec_dram[it * 128:(it + 1) * 128, :],
                                  in_=acc), 16)


def lut_decode_tiles(ctx, block, io, sb, idx_dram, dec_dram, nrows, width):
    for it in range(nrows // 128):
        lut_decode_unit(ctx, block, sb, idx_dram, dec_dram, it, width)


def prep_lutbc(ctx, block, io, sb, psum):
    b = ctx.base()

    @block.sync
    def _(sync, b=b):
        ctx.inc("ld", sync.dma_start(out=sb["rows"][0:1, 0:16], in_=io["lut"][:, :]), 16)

    b = ctx.base()

    @block.tensor
    def _(pe, b=b):
        ctx.wait(pe, "ld", b["ld"])
        ctx.inc("pe", pe.matmul(psum[0][0:128, 0:16], sb["ones_row_f32"][:],
                                sb["rows"][0:1, 0:16], start=True, stop=True))

    b = ctx.base()

    @block.scalar
    def _(act, b=b):
        ctx.wait(act, "pe", b["pe"])
        ctx.inc("act", act.copy(sb["lutbc"][:], psum[0][0:128, 0:16]))


def stage_wdT(ctx, block, io, sb, dec_dram, wdT_dram, ninter):
    noc = ninter // 128
    for oc in range(noc):
        b = ctx.base()

        @block.sync
        def _(sync, oc=oc, b=b):
            ctx.wait(sync, "st", b["st"])
            ctx.inc("ld", sync.dma_start(out=sb["wslab0"][:, 0:DM],
                                         in_=dec_dram[0:DM, oc * 128:(oc + 1) * 128],
                                         transpose=True), 16)

        b = ctx.base()

        @block.sync
        def _(sync, oc=oc, b=b):
            ctx.wait(sync, "ld", b["ld"])
            ctx.inc("st", sync.dma_start(out=wdT_dram[oc * 128:(oc + 1) * 128, :],
                                         in_=sb["wslab0"][:, 0:DM]), 16)


def ffn_gateup(ctx, block, io, sb, psum, gdec, udec, inter_dram, ninter, kind, filler=None):
    import os as _os
    _maxslab = int(_os.environ.get("FFN_MAXSLAB", "9999"))
    noc = ninter // 128
    slabs = []
    o = 0
    while o < noc:
        w = min(4, noc - o)
        slabs.append((o, w))
        o += w
    A = 0.7978845608028654
    B = 0.044715
    gu_mark = [0]
    for (obase, nsub) in slabs[:_maxslab]:
        sw = nsub * 128
        b = ctx.base()

        @block.sync
        def _(sync, obase=obase, nsub=nsub, sw=sw, b=b):
            ctx.wait(sync, "pe", b["pe"])
            ctx.wait(sync, "st", b["st"])
            for c in range(NC):
                ctx.inc("ld", sync.dma_start(
                    out=sb["wslab0"][:, c * TC:c * TC + sw],
                    in_=gdec[obase * 128:obase * 128 + sw, c * 128:(c + 1) * 128],
                    transpose=True), 16)
                ctx.inc("ld", sync.dma_start(
                    out=sb["wslab1"][:, c * TC:c * TC + sw],
                    in_=udec[obase * 128:obase * 128 + sw, c * 128:(c + 1) * 128],
                    transpose=True), 16)

        items = [(ss, tc) for ss in range(nsub) for tc in range(NT)]
        npd = 1 if kind == "silu" else 2
        b = ctx.base()
        pm = gu_mark[0]

        @block.tensor
        def _(pe, obase=obase, items=items, b=b, npd=npd, pm=pm):
            ctx.wait(pe, "ld", b["ld"])
            ctx.wait(pe, "wdma", b["wdma"])
            for m, (ss, tc) in enumerate(items):
                if m < 2:
                    ctx.wait(pe, "dve", pm if pm else b["dve"] + npd * (m - 1))
                else:
                    ctx.wait(pe, "dve", b["dve"] + npd * (m - 1))
                for c in range(NC):
                    i1 = pe.matmul(psum[2 * (m % 2)][:],
                                   sb["wslab0"][:, c * TC + ss * 128:c * TC + (ss + 1) * 128],
                                   sb["xbT"][:, c * T + tc * TC:c * T + (tc + 1) * TC],
                                   start=(c == 0), stop=(c == NC - 1))
                ctx.inc("pe", i1)
                for c in range(NC):
                    i2 = pe.matmul(psum[2 * (m % 2) + 1][:],
                                   sb["wslab1"][:, c * TC + ss * 128:c * TC + (ss + 1) * 128],
                                   sb["xbT"][:, c * T + tc * TC:c * T + (tc + 1) * TC],
                                   start=(c == 0), stop=(c == NC - 1))
                ctx.inc("pe", i2)

        b2 = ctx.base()

        @block.scalar
        def _(act, items=items, kind=kind, b2=b2):
            for m, (ss, tc) in enumerate(items):
                pg = psum[2 * (m % 2)]
                if kind == "silu":
                    ctx.wait(act, "pe", b2["pe"] - 2 * len(items) + 2 * m + 1)
                    ctx.wait(act, "dve", b2["dve"] + m - 1)
                    ctx.inc("act", act.activation((sb["sc1"] if m % 2 == 0 else sb["sc2"])[:],
                                                  pg[:], AF.Silu))
                else:
                    ctx.wait(act, "pe", b2["pe"] - 2 * len(items) + 2 * m + 1)
                    ctx.wait(act, "dve", b2["dve"] + 2 * (m - 1))
                    ctx.inc("act", act.activation((sb["sc1"] if m % 2 == 0 else sb["sc2"])[:],
                                                  pg[:], AF.Square))
                    ctx.wait(act, "dve", b2["dve"] + 2 * m + 1)
                    ctx.inc("act", act.activation((sb["sc1"] if m % 2 == 0 else sb["sc2"])[:],
                                                  (sb["sc1"] if m % 2 == 0 else sb["sc2"])[:],
                                                  AF.Tanh))

        b3 = ctx.base()

        @block.vector
        def _(dve, items=items, kind=kind, b3=b3, obase=obase):
            for m, (ss, tc) in enumerate(items):
                pg = psum[2 * (m % 2)]
                pu = psum[2 * (m % 2) + 1]
                sc = sb["sc1"] if m % 2 == 0 else sb["sc2"]
                ao = sb["ao0"] if m % 2 == 0 else sb["ao1"]
                if kind == "silu":
                    ctx.wait(dve, "act", b3["act"] - len(items) + m + 1)
                    ctx.wait(dve, "pe", b3["pe"] - 2 * len(items) + 2 * m + 2)
                    ctx.wait(dve, "st", b3["st"] + (m - 1) * 16)
                    dve.tensor_tensor(ao[:], sc[:], pu[:], ALU.mult)
                    ctx.dgate(dve, sb)
                else:
                    ctx.wait(dve, "act", b3["act"] - 2 * len(items) + 2 * m + 1)
                    ctx.wait(dve, "pe", b3["pe"] - 2 * len(items) + 2 * m + 2)
                    dve.tensor_scalar(sc[:], sc[:], A * B, A, ALU.mult, ALU.add)
                    ctx.inc("dve", dve.tensor_tensor(sc[:], sc[:], pg[:], ALU.mult))
                    ctx.wait(dve, "act", b3["act"] - 2 * len(items) + 2 * m + 2)
                    ctx.wait(dve, "st", b3["st"] + (m - 1) * 16)
                    dve.tensor_scalar(sc[:], sc[:], 0.5, 0.5, ALU.mult, ALU.add)
                    dve.tensor_tensor(sc[:], sc[:], pg[:], ALU.mult)
                    dve.tensor_tensor(ao[:], sc[:], pu[:], ALU.mult)
                    ctx.dgate(dve, sb)

        gu_mark[0] = ctx.cnt["dve"]
        b4 = ctx.base()
        npq = 2 if kind != "silu" else 1

        @block.sync
        def _(sync, items=items, obase=obase, b4=b4, npq=npq):
            for m, (ss, tc) in enumerate(items):
                ao = sb["ao0"] if m % 2 == 0 else sb["ao1"]
                ctx.wait(sync, "dve", b4["dve"] - npq * len(items) + npq * m + npq)
                orow = (obase + ss) * 128
                ctx.inc("st", sync.dma_start(
                    out=inter_dram[orow:orow + 128, tc * TC:(tc + 1) * TC],
                    in_=ao[:]), 16)
        if filler is not None:
            filler.run(8)

    if filler is not None:
        filler.drain()


def ffn_down_ln(ctx, block, io, sb, psum, wdT_dram, inter_dram, ninter, xT_src,
                store_y, y_bf_dst=None):
    import os as _os
    _maxj = int(_os.environ.get("FFN_MAXJ", "4"))
    _maxoc = int(_os.environ.get("FFN_MAXOC", "9999"))
    _dpart = int(_os.environ.get("FFN_DOWNPART", "4"))
    noc = min(ninter // 128, _maxoc)
    wdt0 = sb["wslab0"][:]
    it0 = sb["prhs"][:]
    for j in range(_maxj):
        for half in range(2):
            pe_base = ctx.cnt["pe"]
            b = ctx.base()

            @block.sync
            def _(sync, j=j, half=half, b=b, pe_base=pe_base):
                ctx.wait(sync, "pe", pe_base)
                ctx.wait(sync, "st", b["st"])
                for oc in range(noc):
                    if oc >= 4:
                        ctx.wait(sync, "pe", pe_base + oc - 3)
                    ctx.inc("ld", sync.dma_start(
                        out=wdt0[:, (oc % 4) * 1024:(oc % 4 + 1) * 1024],
                        in_=wdT_dram[half * 1024:(half + 1) * 1024,
                                     oc * 128:(oc + 1) * 128],
                        transpose=True), 16)
                    ctx.inc("ld", sync.dma_start(
                        out=it0[:, (oc % 4) * TC:(oc % 4 + 1) * TC],
                        in_=inter_dram[oc * 128:(oc + 1) * 128,
                                       j * TC:(j + 1) * TC]), 16)
                ctx.inc("ld", sync.dma_start(out=sb["dly"][0:1, 0:64],
                                             in_=xT_src[0:1, 0:64]), 16)

            b2 = ctx.base()

            @block.tensor
            def _(pe, j=j, half=half, b2=b2):
                ctx.wait(pe, "dve", b2["dve"])
                ctx.wait(pe, "act", b2["act"])
                ctx.wait(pe, "st", b2["st"])
                total = 32 * noc + 16
                for oc in range(noc):
                    ctx.wait(pe, "ld", min(b2["ld"], b2["ld"] - total + 32 * (oc + 2)))
                    for u in range(8):
                        ins = pe.matmul(psum[(u + 2) % 8][:],
                                        wdt0[:, (oc % 4) * 1024 + u * 128:(oc % 4) * 1024 + (u + 1) * 128],
                                        it0[:, (oc % 4) * TC:(oc % 4 + 1) * TC],
                                        start=(oc == 0), stop=(oc == noc - 1))
                    ctx.inc("pe", ins)

            if _dpart <= 1:
                continue
            for w in range(2):
                b3 = ctx.base()

                @block.sync
                def _(sync, j=j, half=half, w=w, b3=b3):
                    ctx.wait(sync, "dve", b3["dve"])
                    for uu in range(4):
                        u = w * 4 + uu
                        q = half * 8 + u
                        ctx.inc("ld", sync.dma_start(
                            out=sb["xt_ring"][:, uu * TC:(uu + 1) * TC],
                            in_=xT_src[q * 128:(q + 1) * 128, j * TC:(j + 1) * TC]), 16)
                    ctx.inc("ld", sync.dma_start(out=sb["dly"][0:1, 0:32],
                                                 in_=xT_src[0:1, 0:32]), 16)

                b4 = ctx.base()

                @block.vector
                def _(dve, j=j, half=half, w=w, b4=b4):
                    ctx.wait(dve, "ld", b4["ld"])
                    ctx.wait(dve, "pe", b4["pe"])
                    for uu in range(4):
                        u = w * 4 + uu
                        q = half * 8 + u
                        ctx.inc("dve", dve.tensor_tensor(
                            sb["resid"][:, q * TC:(q + 1) * TC], psum[(u + 2) % 8][:],
                            sb["xt_ring"][:, uu * TC:(uu + 1) * TC], ALU.add))

        if _dpart <= 2:
            continue
        b5 = ctx.base()
        pe_stats_base = ctx.cnt["pe"]

        @block.scalar
        def _(act, j=j, b5=b5, pe_stats_base=pe_stats_base):
            for q in range(NC):
                ctx.wait(act, "dve", b5["dve"] - 16 + q + 1)
                if q >= 2:
                    ctx.wait(act, "pe", pe_stats_base + (q - 2) + 1)
                ctx.inc("act", act.activation(
                    (sb["sc1"] if q % 2 == 0 else sb["sc2"])[:],
                    sb["resid"][:, q * TC:(q + 1) * TC], AF.Square))

        b6 = ctx.base()

        @block.tensor
        def _(pe, j=j, b6=b6):
            ctx.wait(pe, "dve", b6["dve"])
            for q in range(NC):
                ctx.wait(pe, "act", b6["act"] - 16 + q + 1)
                i1 = pe.matmul(psum[6][0:1, :], sb["ones_col_f32"][:],
                               sb["resid"][:, q * TC:(q + 1) * TC],
                               start=(q == 0), stop=(q == NC - 1))
                i2 = pe.matmul(psum[7][0:1, :], sb["ones_col_f32"][:],
                               (sb["sc1"] if q % 2 == 0 else sb["sc2"])[:],
                               start=(q == 0), stop=(q == NC - 1))
                ctx.inc("pe", i2)

        rows = sb["rows"]
        b = ctx.base()

        @block.vector
        def _(dve, j=j, b=b):
            ctx.wait(dve, "pe", b["pe"])
            ctx.fence(dve, "dve", dve.tensor_scalar(rows[0:1, 0:TC], psum[6][0:1, :],
                                                    1.0 / DM, None, ALU.mult))
            ctx.fence(dve, "dve", dve.tensor_scalar(rows[0:1, TC:2 * TC],
                                                    psum[7][0:1, :], 1.0 / DM, None,
                                                    ALU.mult))
            ctx.fence(dve, "dve", dve.tensor_tensor(rows[0:1, 2 * TC:3 * TC],
                                                    rows[0:1, 0:TC], rows[0:1, 0:TC],
                                                    ALU.mult))
            ctx.inc("dve", dve.tensor_tensor(rows[0:1, TC:2 * TC],
                                             rows[0:1, TC:2 * TC],
                                             rows[0:1, 2 * TC:3 * TC], ALU.subtract))

        b = ctx.base()

        @block.scalar
        def _(act, j=j, b=b):
            ctx.wait(act, "dve", b["dve"])
            ctx.inc("act", act.activation(rows[0:1, 2 * TC:3 * TC],
                                          rows[0:1, TC:2 * TC], AF.Sqrt,
                                          bias=sb["eps_f32"][0:1, 0:1]))

        b = ctx.base()

        @block.vector
        def _(dve, j=j, b=b):
            ctx.wait(dve, "act", b["act"])
            ctx.fence(dve, "dve", dve.reciprocal(rows[0:1, TC:2 * TC],
                                                 rows[0:1, 2 * TC:3 * TC]))
            ctx.inc("dve", dve.tensor_tensor(rows[0:1, 2 * TC:3 * TC], rows[0:1, 0:TC],
                                             rows[0:1, TC:2 * TC], ALU.mult))

        b = ctx.base()

        @block.tensor
        def _(pe, j=j, b=b):
            ctx.wait(pe, "dve", b["dve"])
            ctx.inc("pe", pe.matmul(psum[4][:], sb["ones_row_f32"][:],
                                    rows[0:1, TC:2 * TC], start=True, stop=True))
            ctx.inc("pe", pe.matmul(psum[5][:], sb["ones_row_f32"][:],
                                    rows[0:1, 2 * TC:3 * TC], start=True, stop=True))

        b = ctx.base()

        @block.scalar
        def _(act, j=j, b=b):
            ctx.wait(act, "pe", b["pe"])
            act.copy(sb["rstd_bc"][:], psum[4][:])
            ctx.inc("act", act.copy(sb["mrow_bc"][:], psum[5][:]))

        if _dpart <= 3:
            continue
        b = ctx.base()
        npq = 2 if y_bf_dst is not None else 1

        @block.vector
        def _(dve, j=j, b=b, npq=npq):
            ctx.wait(dve, "act", b["act"])
            for q in range(NC):
                slot = q % 2
                if q >= 2:
                    ctx.wait(dve, "st", b["st"] + (q - 1) * 16)
                dve.tensor_tensor(sb["sc1"][:], sb["resid"][:, q * TC:(q + 1) * TC],
                                  sb["rstd_bc"][:], ALU.mult)
                dve.tensor_tensor(
                    sb["y_ring"][:, slot * TC:(slot + 1) * TC],
                    sb["sc1"][:], sb["mrow_bc"][:], ALU.subtract)
                if y_bf_dst is not None:
                    dve.tensor_copy(
                        y_bf_dst[:, q * T + j * TC:q * T + (j + 1) * TC],
                        sb["y_ring"][:, slot * TC:(slot + 1) * TC])
                ctx.dgate(dve, sb)

        b = ctx.base()

        @block.sync
        def _(sync, j=j, b=b, npq=npq):
            for q in range(NC):
                slot = q % 2
                ctx.wait(sync, "dve", b["dve"] - NC + q + 1)
                ctx.inc("st", store_y(sync, j, q,
                                      sb["y_ring"][:, slot * TC:(slot + 1) * TC]), 16)


def build_l1_ffn(ctx, block, io, sb, psum, stage=9, skip_decode=False):
    @block.gpsimd
    def _(g):
        for c in range(NC):
            ctx.inc("wdma", g.dma_start(out=sb["xbT"][:, c * T:(c + 1) * T],
                                        in_=io["xT_in"][c * 128:(c + 1) * 128, :]), 16)

    if not skip_decode:
        prep_lutbc(ctx, block, io, sb, psum)
        lut_decode_tiles(ctx, block, io, sb, io["gate_idx"], io["g1_dec"], 5504, DM)
        if stage <= 1:
            return
        lut_decode_tiles(ctx, block, io, sb, io["up_idx"], io["u1_dec"], 5504, DM)
        lut_decode_tiles(ctx, block, io, sb, io["down_idx"], io["d1_dec"], DM, 5504)
        if stage <= 2:
            return
    ffn_gateup(ctx, block, io, sb, psum, io["g1_dec"], io["u1_dec"],
               io["inter_dram"], 5504, "silu")
    if stage <= 3:
        return

    def store_y(sync, j, q, ysl):
        return sync.dma_start(out=io["xT_out"][q * 128:(q + 1) * 128, j * TC:(j + 1) * TC],
                              in_=ysl)

    ffn_down_ln(ctx, block, io, sb, psum, io["d1_dec"], io["inter_dram"], 5504,
                io["xT_in"], store_y, y_bf_dst=io.get("y_bf_dst"))


def build_program_l1_ffn(stage=9):
    nc = bass.Bass("TRN2", target_bir_lowering=False, debug=False)
    io = {}
    io["xT_in"] = nc.dram_tensor("xT_in", [DM, T], F32, kind="ExternalInput").ap()
    io["lut"] = nc.dram_tensor("lut", [1, 16], F32, kind="ExternalInput").ap()
    io["gate_idx"] = nc.dram_tensor("gate_idx", [5504, DM], I32, kind="ExternalInput").ap()
    io["up_idx"] = nc.dram_tensor("up_idx", [5504, DM], I32, kind="ExternalInput").ap()
    io["down_idx"] = nc.dram_tensor("down_idx", [DM, 5504], I32, kind="ExternalInput").ap()
    io["xT_out"] = nc.dram_tensor("xT_out", [DM, T], F32, kind="ExternalOutput").ap()
    io["g1_dec"] = nc.dram_tensor("g1_dec", [5504, DM], BF16,
                                  kind="ExternalOutput" if stage == 1 else "Internal").ap()
    io["u1_dec"] = nc.dram_tensor("u1_dec", [5504, DM], BF16, kind="Internal").ap()
    io["d1_dec"] = nc.dram_tensor("d1_dec", [DM, 5504], BF16, kind="Internal").ap()
    io["wdT1_dram"] = nc.dram_tensor("wdT1_dram", [5504, DM], BF16,
                                     kind="ExternalOutput" if stage == 2 else "Internal").ap()
    io["inter_dram"] = nc.dram_tensor("inter_dram", [5504, T], BF16,
                                      kind="ExternalOutput" if stage == 3 else "Internal").ap()
    cap = declare_consts(nc)
    sb, psum = alloc_common(nc)
    ctx = Ctx(nc)
    with nc.Block() as block:
        load_consts(ctx, block, cap, sb)
        build_l1_ffn(ctx, block, io, sb, psum, stage=stage)
        finish(ctx, block)
    return nc


def _run_l1_ffn(xT_all, weights, stage=9):
    nc = build_program_l1_ffn(stage=stage)
    consts = host_constants()
    in_maps = []
    for cid in range(8):
        m = {"xT_in": np.ascontiguousarray(xT_all[cid]),
             "lut": np.ascontiguousarray(weights["L1_ffn_lut"].reshape(1, 16)),
             "gate_idx": weights["L1_ffn_gate_idx"], "up_idx": weights["L1_ffn_up_idx"],
             "down_idx": weights["L1_ffn_down_idx"]}
        for k, v in consts.items():
            m["c_" + k] = v
        in_maps.append(m)
    res = run_bass_kernel_spmd(nc, in_maps, core_ids=list(range(8)))
    key = {1: "g1_dec", 2: "wdT1_dram", 3: "inter_dram"}.get(stage, "xT_out")
    return np.stack([res.results[i][key] for i in range(8)])



def nvfp4_scl_unit(ctx, block, sb, sc_dram, it, nch):
    bsc = ctx.base()

    @block.gpsimd
    def _(g, it=it, bsc=bsc):
        ctx.wait(g, "dve", bsc["dve"])
        ctx.inc("fld", g.dma_start(out=sb["scl"][:, 0:64 * nch],
                                   in_=sc_dram[it * 128:(it + 1) * 128, :]), 16)
        ctx.inc("fld", g.dma_start(out=sb["dly"][0:1, 0:32],
                                   in_=sc_dram[0:1, 0:32]), 16)


def nvfp4_chunk_unit(ctx, block, sb, pk_dram, dec_dram, it, ch, nch, alt=False):
    pv = sb["prhs"][:].bitcast(U16)
    bv = sb["prhs"][:]
    pk = pv[:, 0:1024]
    bs4 = pv[:, 1024:2048]
    m_lo = pv[:, 2048:3072]
    m_hi = pv[:, 3072:4096]
    g_lo = bv[:, 4096:5120]
    g_hi = bv[:, 5120:6144]
    t1 = bv[:, 6144:7168]
    d0 = bv[:, 7168:8192]
    if alt:
        t2 = sb["y_ring"][:].bitcast(BF16)[:, 0:1024]
        wout = sb["xt_ring"][:].bitcast(BF16)[:, 0:2048]
    else:
        t2 = sb["un1"][:, 2048:3072]
        wout = sb["un1"][:, 0:2048]
    b = ctx.base()

    @block.gpsimd
    def _(g, it=it, ch=ch, b=b):
        ctx.wait(g, "dve", b["dve"])
        ctx.inc("fld", g.dma_start(
            out=pk,
            in_=pk_dram[it * 128:(it + 1) * 128, ch * 1024:(ch + 1) * 1024]), 16)
        ctx.inc("fld", g.dma_start(out=sb["dly"][0:1, 32:64].bitcast(U16),
                                   in_=pk_dram[0:1, 0:64]), 16)

    b = ctx.base()

    @block.vector
    def _(dve, it=it, ch=ch, b=b, nch=nch):
        ctx.wait(dve, "fld", b["fld"])
        ctx.wait(dve, "st", b["st"])
        if ch == 0:
            ctx.fence(dve, "dve", dve.tensor_scalar(
                sb["scl"][:, 0:64 * nch], sb["scl"][:, 0:64 * nch],
                -0.125, None, ALU.mult))
        sexp = sb["scl"][:, ch * 64:(ch + 1) * 64].unsqueeze(2) \
            .broadcast_to([128, 64, 16])
        dve.tensor_scalar(bs4, pk, 4, None, ALU.logical_shift_right)
        dve.tensor_single_scalar(m_lo, pk, 7, ALU.bitwise_and)
        dve.tensor_single_scalar(m_hi, bs4, 7, ALU.bitwise_and)
        gu_lo = g_lo.bitcast(U16)
        gu_hi = g_hi.bitcast(U16)
        dve.tensor_single_scalar(gu_lo, pk, 8, ALU.bitwise_and)
        dve.tensor_single_scalar(gu_hi, bs4, 8, ALU.bitwise_and)
        dve.tensor_scalar(g_lo, gu_lo, 1.0, -4.0, ALU.mult, ALU.add)
        dve.tensor_scalar(g_hi, gu_hi, 1.0, -4.0, ALU.mult, ALU.add)
        for (mm, gg, off) in ((m_lo, g_lo, 0), (m_hi, g_hi, 1)):
            dve.tensor_scalar(t1, mm, 2.0, -4.0, ALU.mult, ALU.add)
            dve.tensor_tensor(t2, mm, t1, ALU.max)
            dve.tensor_scalar(t1, mm, 4.0, -16.0, ALU.mult, ALU.add)
            dve.tensor_tensor(t2, t2, t1, ALU.max)
            dve.tensor_tensor(d0, t2, gg, ALU.mult)
            dve.tensor_tensor(
                wout[:, off:2048:2].rearrange("p (a b) -> p a b", b=16),
                d0[:, :].rearrange("p (a b) -> p a b", b=16),
                sexp, ALU.mult)
        ctx.dgate(dve, sb)

    b = ctx.base()

    @block.gpsimd
    def _(g, it=it, ch=ch, b=b):
        ctx.wait(g, "dve", b["dve"])
        ctx.inc("st", g.dma_start(
            out=dec_dram[it * 128:(it + 1) * 128, ch * 2048:(ch + 1) * 2048],
            in_=wout), 16)


def nvfp4_decode_tiles(ctx, block, io, sb, pk_dram, sc_dram, dec_dram, nrows, kbytes,
                       alt=False):
    ntile = nrows // 128
    nch = kbytes // 1024
    for it in range(ntile):
        nvfp4_scl_unit(ctx, block, sb, sc_dram, it, nch)
        for ch in range(nch):
            nvfp4_chunk_unit(ctx, block, sb, pk_dram, dec_dram, it, ch, nch, alt=alt)


def build_l0_ffn(ctx, block, io, sb, psum, stage=9, skip_decode=False, filler=None):
    @block.gpsimd
    def _(g):
        for c in range(NC):
            ctx.inc("wdma", g.dma_start(out=sb["xbT"][:, c * T:(c + 1) * T],
                                        in_=io["xT_in"][c * 128:(c + 1) * 128, :]), 16)

    if not skip_decode:
        nvfp4_decode_tiles(ctx, block, io, sb, io["gate_W_packed"], io["gate_scales_n"],
                           io["g0_dec"], 16384, 1024)
        if stage <= 1:
            return
        nvfp4_decode_tiles(ctx, block, io, sb, io["up_W_packed"], io["up_scales_n"],
                           io["u0_dec"], 16384, 1024)
        nvfp4_decode_tiles(ctx, block, io, sb, io["down_W_packed"], io["down_scales_n"],
                           io["d0_dec"], 2048, 8192)
        if stage <= 2:
            return
    ffn_gateup(ctx, block, io, sb, psum, io["g0_dec"], io["u0_dec"],
               io["inter0_dram"], 16384, "gelu", filler=filler)
    if stage <= 3:
        return

    def store_y(sync, j, q, ysl):
        return sync.dma_start(out=io["xT_out"][q * 128:(q + 1) * 128, j * TC:(j + 1) * TC],
                              in_=ysl)

    ffn_down_ln(ctx, block, io, sb, psum, io["d0_dec"], io["inter0_dram"], 16384,
                io["xT_in"], store_y, y_bf_dst=io.get("y_bf_dst"))


def build_program_l0_ffn(stage=9):
    nc = bass.Bass("TRN2", target_bir_lowering=False, debug=False)
    io = {}
    io["xT_in"] = nc.dram_tensor("xT_in", [DM, T], F32, kind="ExternalInput").ap()
    io["gate_W_packed"] = nc.dram_tensor("gate_W_packed", [16384, 1024], I32,
                                         kind="ExternalInput").ap()
    io["gate_scales_n"] = nc.dram_tensor("gate_scales_n", [16384, 64], F32,
                                         kind="ExternalInput").ap()
    io["up_W_packed"] = nc.dram_tensor("up_W_packed", [16384, 1024], I32,
                                       kind="ExternalInput").ap()
    io["up_scales_n"] = nc.dram_tensor("up_scales_n", [16384, 64], F32,
                                       kind="ExternalInput").ap()
    io["down_W_packed"] = nc.dram_tensor("down_W_packed", [2048, 8192], I32,
                                         kind="ExternalInput").ap()
    io["down_scales_n"] = nc.dram_tensor("down_scales_n", [2048, 512], F32,
                                         kind="ExternalInput").ap()
    io["xT_out"] = nc.dram_tensor("xT_out", [DM, T], F32, kind="ExternalOutput").ap()
    io["g0_dec"] = nc.dram_tensor("g0_dec", [16384, DM], BF16,
                                  kind="ExternalOutput" if stage == 1 else "Internal").ap()
    io["u0_dec"] = nc.dram_tensor("u0_dec", [16384, DM], BF16, kind="Internal").ap()
    io["d0_dec"] = nc.dram_tensor("d0_dec", [DM, 16384], BF16, kind="Internal").ap()
    io["wdT0_dram"] = nc.dram_tensor("wdT0_dram", [16384, DM], BF16,
                                     kind="ExternalOutput" if stage == 2 else "Internal").ap()
    io["inter0_dram"] = nc.dram_tensor("inter0_dram", [16384, T], BF16,
                                       kind="ExternalOutput" if stage == 3 else "Internal").ap()
    cap = declare_consts(nc)
    sb, psum = alloc_common(nc)
    ctx = Ctx(nc)
    with nc.Block() as block:
        load_consts(ctx, block, cap, sb)
        build_l0_ffn(ctx, block, io, sb, psum, stage=stage)
        finish(ctx, block)
    return nc


def _run_l0_ffn(xT_all, weights, stage=9):
    nc = build_program_l0_ffn(stage=stage)
    consts = host_constants()
    in_maps = []
    for cid in range(8):
        m = {"xT_in": np.ascontiguousarray(xT_all[cid]),
             "gate_W_packed": weights["L0_ffn_gate_W_packed"],
             "gate_scales_n": weights["L0_ffn_gate_scales"],
             "up_W_packed": weights["L0_ffn_up_W_packed"],
             "up_scales_n": weights["L0_ffn_up_scales"],
             "down_W_packed": weights["L0_ffn_down_W_packed"],
             "down_scales_n": weights["L0_ffn_down_scales"]}
        for k, v in consts.items():
            m["c_" + k] = v
        in_maps.append(m)
    res = run_bass_kernel_spmd(nc, in_maps, core_ids=list(range(8)))
    key = {1: "g0_dec", 2: "wdT0_dram", 3: "inter0_dram"}.get(stage, "xT_out")
    return np.stack([res.results[i][key] for i in range(8)])




def barrier_all(ctx, block):
    c = ctx.base()
    for name in ("sync", "tensor", "vector", "scalar", "gpsimd"):
        def body(eng, c=c):
            for s in ("ld", "wdma", "st", "pe", "act", "dve", "fld"):
                ctx.wait(eng, s, c[s])
        getattr(block, name)(body)


def build_program_fused(dbg_inter=False):
    nc = bass.Bass("TRN2", target_bir_lowering=False, debug=False)
    io = {}
    io["x0T"] = nc.dram_tensor("x0T", [DM, T], F32, kind="ExternalInput").ap()
    for n, shp in (("Wq", [DM, DM]), ("Wk", [512, DM]), ("Wv", [512, DM]),
                   ("Wo", [DM, DM]), ("Wgslc", [16, DM]), ("Wgswa", [16, DM]),
                   ("Wd", [512, DM]), ("Wq1", [DM, DM]), ("Wk1", [DM, 512]),
                   ("Wv1", [DM, 512]), ("Wo1", [DM, DM]), ("lut", [1, 16])):
        io[n] = nc.dram_tensor(n, shp, F32, kind="ExternalInput").ap()
    for n, shp in (("gate_W_packed", [16384, 1024]), ("up_W_packed", [16384, 1024]),
                   ("down_W_packed", [2048, 8192]), ("gate_idx", [5504, DM]),
                   ("up_idx", [5504, DM]), ("down_idx", [DM, 5504])):
        io[n] = nc.dram_tensor(n, shp, I32, kind="ExternalInput").ap()
    for n, shp in (("gate_scales_n", [16384, 64]), ("up_scales_n", [16384, 64]),
                   ("down_scales_n", [2048, 512])):
        io[n] = nc.dram_tensor(n, shp, F32, kind="ExternalInput").ap()
    io["yT"] = nc.dram_tensor("yT", [DM, T], F32, kind="ExternalOutput").ap()
    ik = "ExternalOutput" if dbg_inter else "Internal"
    io["x1T"] = nc.dram_tensor("x1T", [DM, T], F32, kind=ik).ap()
    io["x2T"] = nc.dram_tensor("x2T", [DM, T], F32, kind=ik).ap()
    io["x3T"] = nc.dram_tensor("x3T", [DM, T], F32, kind=ik).ap()
    for n, shp in (("wq_bf", [DM, DM]), ("wk_bf", [512, DM]), ("wv_bf", [512, DM]),
                   ("wo_bf", [DM, DM]), ("wgs_bf", [32, DM]), ("attn_bf", [DM, T]),
                   ("wd_bf", [512, DM]), ("wq1_bf", [DM, DM]), ("wk1_bf", [DM, 512]),
                   ("wv1_bf", [DM, 512]), ("wo1_bf", [DM, DM]),
                   ("g0_dec", [16384, DM]), ("u0_dec", [16384, DM]),
                   ("d0_dec", [DM, 16384]), ("wdT0_dram", [16384, DM]),
                   ("inter0_dram", [16384, T]),
                   ("g1_dec", [5504, DM]), ("u1_dec", [5504, DM]),
                   ("d1_dec", [DM, 5504]), ("wdT1_dram", [5504, DM]),
                   ("inter_dram", [5504, T])):
        io[n] = nc.dram_tensor(n, shp, BF16, kind="Internal").ap()
    cap = declare_consts(nc)
    sb, psum = alloc_common(nc)
    ctx = Ctx(nc)
    with nc.Block() as block:
        load_consts(ctx, block, cap, sb)
        prep_lutbc(ctx, block, io, sb, psum)

        fa = Fillers()
        for mat, scm, dec in (("gate_W_packed", "gate_scales_n", "g0_dec"),
                              ("up_W_packed", "up_scales_n", "u0_dec")):
            for it in range(128):
                def unit(it=it, mat=mat, scm=scm, dec=dec):
                    nvfp4_scl_unit(ctx, block, sb, io[scm], it, 1)
                    nvfp4_chunk_unit(ctx, block, sb, io[mat], io[dec], it, 0, 1,
                                     alt=True)
                fa.q.append(unit)
        fg = Fillers()
        for it in range(16):
            def unit_s(it=it):
                nvfp4_scl_unit(ctx, block, sb, io["down_scales_n"], it, 8)
            fg.q.append(unit_s)
            for ch in range(8):
                def unit_c(it=it, ch=ch):
                    nvfp4_chunk_unit(ctx, block, sb, io["down_W_packed"],
                                     io["d0_dec"], it, ch, 8, alt=False)
                fg.q.append(unit_c)
        for mat, dec, nrows, width in (("gate_idx", "g1_dec", 5504, DM),
                                       ("up_idx", "u1_dec", 5504, DM),
                                       ("down_idx", "d1_dec", DM, 5504)):
            for it in range(nrows // 128):
                def unit_l(it=it, mat=mat, dec=dec, width=width):
                    lut_decode_unit(ctx, block, sb, io[mat], io[dec], it, width)
                fg.q.append(unit_l)

        io1 = dict(io); io1["xT_in"] = io["x0T"]; io1["xT_out"] = io["x1T"]
        build_l0_attn(ctx, block, io1, sb, psum, filler=fa)
        barrier_all(ctx, block)
        io2 = dict(io); io2["xT_in"] = io["x1T"]; io2["xT_out"] = io["x2T"]
        build_l0_ffn(ctx, block, io2, sb, psum, skip_decode=True, filler=fg)
        barrier_all(ctx, block)
        io3 = dict(io); io3["xT_in"] = io["x2T"]; io3["xT_out"] = io["x3T"]
        build_l1_attn(ctx, block, io3, sb, psum)
        barrier_all(ctx, block)
        io4 = dict(io); io4["xT_in"] = io["x3T"]; io4["xT_out"] = io["yT"]
        build_l1_ffn(ctx, block, io4, sb, psum, skip_decode=True)
        finish(ctx, block)
    return nc


def kernel(**inputs):
    x = inputs["L0_attn_hidden_states"]
    xT = np.ascontiguousarray(np.transpose(x, (0, 2, 1)).astype(np.float32, copy=False))
    nc = build_program_fused()
    consts = host_constants()
    in_maps = []
    for cid in range(8):
        m = {"x0T": xT[cid],
             "Wq": inputs["L0_attn_Wq"], "Wk": inputs["L0_attn_Wk"],
             "Wv": inputs["L0_attn_Wv"], "Wo": inputs["L0_attn_Wo"],
             "Wgslc": inputs["L0_attn_Wgslc"], "Wgswa": inputs["L0_attn_Wgswa"],
             "gate_W_packed": inputs["L0_ffn_gate_W_packed"],
             "gate_scales_n": inputs["L0_ffn_gate_scales"],
             "up_W_packed": inputs["L0_ffn_up_W_packed"],
             "up_scales_n": inputs["L0_ffn_up_scales"],
             "down_W_packed": inputs["L0_ffn_down_W_packed"],
             "down_scales_n": inputs["L0_ffn_down_scales"],
             "Wd": inputs["L1_attn_Wd"], "Wq1": inputs["L1_attn_Wq"],
             "Wk1": inputs["L1_attn_Wk"], "Wv1": inputs["L1_attn_Wv"],
             "Wo1": inputs["L1_attn_Wo"],
             "lut": np.ascontiguousarray(inputs["L1_ffn_lut"].reshape(1, 16)),
             "gate_idx": inputs["L1_ffn_gate_idx"], "up_idx": inputs["L1_ffn_up_idx"],
             "down_idx": inputs["L1_ffn_down_idx"]}
        for k, v in consts.items():
            m["c_" + k] = v
        in_maps.append(m)
    res = run_bass_kernel_spmd(nc, in_maps, core_ids=list(range(8)))
    yT = np.stack([res.results[i]["yT"] for i in range(8)])
    return np.ascontiguousarray(np.transpose(yT, (0, 2, 1)))

